# version 20
# baseline (speedup 1.0000x reference)
import numpy as np
from contextlib import ExitStack

import concourse.bass as bass
import concourse.mybir as mybir
from concourse.bass import IndirectOffsetOnAxis
from concourse.bass_utils import run_bass_kernel_spmd

F32 = mybir.dt.float32
BF16 = mybir.dt.bfloat16
U32 = mybir.dt.uint32
AF = mybir.ActivationFunctionType
ALU = mybir.AluOpType
AX = mybir.AxisListType

N = 4096
D = 1024
P1_NW = 4
F2DBG = False
P1_PART = "post"
H = 16
DH = 64
IC = 256
NCH = N // IC
NJT = N // 128
NTT = N // 128
E = 32
CAP = 256
PPAD = E * CAP
PTOT = PPAD + 128
IFF = 2048
LN_EPS = 1e-5
NCORE = 8
ENGINES = ("tensor", "vector", "scalar", "gpsimd", "sync")


def _table_id(nc, name):
    from concourse.hw_specs import get_activation_tables
    return list(get_activation_tables(nc.m.arch).keys()).index(name)


def host_consts():
    c = {}
    c["ident"] = np.eye(128, dtype=np.float32)
    c["onesr"] = np.ones((1, 128), dtype=np.float32)
    c["ut128"] = np.triu(np.ones((128, 128), dtype=np.float32))
    hma = np.zeros((128, 128), dtype=np.float32)
    for cc in range(8):
        for r in range(128):
            hma[r, cc * 16 + 2 * cc + r // 64] = 1.0
    c["hmall"] = hma
    c["io32"] = np.tile(np.arange(32, dtype=np.float32), (128, 1))
    c["cem1"] = np.tile(CAP * np.arange(32, dtype=np.float32), (128, 1))
    c["onesc"] = np.ones((128, 1), dtype=np.float32)
    c["tloc"] = (PPAD + np.arange(128, dtype=np.float32))[:, None]
    return c


class Sched:

    def __init__(self, nc):
        self.nc = nc
        self.steps = {e: [] for e in ENGINES}
        self.n = {}
        self.sems = {}

    def on(self, eng, fn):
        self.steps[eng].append(fn)

    def inc(self, k, amt=1):
        self.n[k] = self.n.get(k, 0) + amt
        return self.n[k]

    def get(self, k):
        return self.n.get(k, 0)

    def emit(self, blk):
        for eng in ENGINES:
            steps = self.steps[eng]

            def mk(steps):
                def _run(e):
                    for fn in steps:
                        fn(e)
                return _run
            getattr(blk, eng)(mk(steps))

    def barrier(self):
        snap = {}
        for eng in ENGINES:
            k = "b_" + eng
            tgt = self.inc(k, 1)
            sem = self.sems[k]

            def f(e, sem=sem):
                e.drain()
                e.nop().then_inc(sem, 1)
            self.on(eng, f)
            snap[k] = tgt
        for eng in ENGINES:
            def w(e, snap=dict(snap), sems=self.sems):
                for k, v in snap.items():
                    e.wait_ge(sems[k], v)
            self.on(eng, w)


def build_nc(taps=(), upto="all"):
    nc = bass.Bass("TRN2", target_bir_lowering=False)
    sch = Sched(nc)

    def dram(name, shape, dtype, kind="Internal"):
        k = "ExternalOutput" if name in taps else kind
        return nc.dram_tensor(name, shape, dtype, kind=k).ap()

    x_in = nc.dram_tensor("x", [N, D], F32, kind="ExternalInput").ap()
    Wq = nc.dram_tensor("Wq", [D, D], F32, kind="ExternalInput").ap()
    Wk = nc.dram_tensor("Wk", [D, D], F32, kind="ExternalInput").ap()
    Wv = nc.dram_tensor("Wv", [D, D], F32, kind="ExternalInput").ap()
    Wproj = nc.dram_tensor("Wproj", [D, D], F32, kind="ExternalInput").ap()
    bproj = nc.dram_tensor("bproj", [D], F32, kind="ExternalInput").ap()
    Wpre = nc.dram_tensor("Wpre", [D, D], F32, kind="ExternalInput").ap()
    bpre = nc.dram_tensor("bpre", [D], F32, kind="ExternalInput").ap()
    pi_in = nc.dram_tensor("pi", [1], F32, kind="ExternalInput").ap()
    gate_w = nc.dram_tensor("gate_w", [D, E], F32, kind="ExternalInput").ap()
    sh_g = nc.dram_tensor("sh_g", [D, IFF], F32, kind="ExternalInput").ap()
    sh_u = nc.dram_tensor("sh_u", [D, IFF], F32, kind="ExternalInput").ap()
    sh_d = nc.dram_tensor("sh_d", [IFF, D], F32, kind="ExternalInput").ap()
    r_g = nc.dram_tensor("r_g", [E, D, IFF], F32, kind="ExternalInput").ap()
    r_u = nc.dram_tensor("r_u", [E, D, IFF], F32, kind="ExternalInput").ap()
    r_d = nc.dram_tensor("r_d", [E, IFF, D], F32, kind="ExternalInput").ap()
    ci = {}
    for k, v in host_consts().items():
        ci[k] = nc.dram_tensor("c_" + k, list(v.shape), F32,
                               kind="ExternalInput").ap()
    out_d = nc.dram_tensor("out", [N, D], F32, kind="ExternalOutput").ap()

    ln1_d = dram("ln1_d", [N, D], BF16)
    kt_d = dram("kt_d", [D, N], BF16)
    qa_d = dram("qa_d", [D, N], BF16)
    pre_d = dram("pre_d", [D, N], BF16)
    xmid_d = dram("xmid_d", [N, D], BF16)
    ln2_d = dram("ln2_d", [N, D], BF16)
    sorted_d = dram("sorted_d", [PTOT, D], BF16)
    dexp_d = dram("dexp_d", [PTOT, D], BF16)
    f2dbg_d = dram("f2dbg", [128, 12288], BF16) if F2DBG else None

    es_top = ExitStack()

    def sbt(esx, name, shape, dtype, **kw):
        return esx.enter_context(nc.sbuf_tensor(name, shape, dtype, **kw))

    ident = sbt(es_top, "ident", [128, 128], BF16)
    sid = sbt(es_top, "sid", [128, 128], BF16)
    onesr = sbt(es_top, "onesr", [1, 128], F32)
    ut128 = sbt(es_top, "ut128", [128, 128], BF16)
    hmall = sbt(es_top, "hmall", [128, 128], BF16)
    io32 = sbt(es_top, "io32", [128, 32], BF16)
    cem1 = sbt(es_top, "cem1", [128, 32], BF16)
    onesc = sbt(es_top, "onesc", [128, 1], BF16)
    onesrb = sbt(es_top, "onesrb", [1, 128], BF16)
    identf = sbt(es_top, "identf", [128, 128], F32)
    smT = sbt(es_top, "smT", [128, 2], F32)
    tloc = smT[:, 0:1]
    sprev = smT[:, 1:2]
    bprec = sbt(es_top, "bprec", [128, 8], F32)
    smA = sbt(es_top, "smA", [128, 6], F32)
    uall = sbt(es_top, "uall", [16, N], BF16)
    p_all = sbt(es_top, "p_all", [128, NTT], U32)
    sc_all = sbt(es_top, "sc_all", [128, NTT], F32)

    dma_sem_names = [
        "d_ini", "d_ini2", "d_xb0", "d_xb1", "d_st0", "d_st1",
        "d_w0", "d_w1", "d_sg0", "d_sg1", "d_trl", "d_p1st",
        "d_kt", "d_q", "d_pj0", "d_pj1", "d_wp", "d_x", "d_xm",
        "d_l2", "d_sct", "d_sx0", "d_sx1", "d_g0", "d_g1", "d_g2", "d_g3", "d_g4", "d_g5", "d_g6", "d_g7", "d_g8", "d_g9", "d_g10", "d_g11", "d_g12", "d_g13", "d_g14", "d_g15",
        "d_u0", "d_u1", "d_u2", "d_u3", "d_u4", "d_u5", "d_u6", "d_u7", "d_u8", "d_u9", "d_u10", "d_u11", "d_u12", "d_u13", "d_u14", "d_u15", "d_d0", "d_d1", "d_d2", "d_d3", "d_d4", "d_d5", "d_d6", "d_d7", "d_d8", "d_d9", "d_d10", "d_d11", "d_d12", "d_d13", "d_d14", "d_d15",
        "d_dst", "d_shw", "d_dbg",
        "d_ga", "d_xt", "d_out"]
    for nm in (["pe", "ve", "se"] + ["b_" + e for e in ENGINES]
               + dma_sem_names):
        sch.sems[nm] = es_top.enter_context(nc.semaphore(nm))
    SS = sch.sems
    LNEXP = "natural_log_exp_and_others"
    SILUS = "silu_and_others"

    def s_table(name):
        def f(s, name=name):
            s.add_instruction(mybir.InstLoadActFuncSet(
                name=nc.get_next_instruction_name(),
                act_func_set_id=_table_id(nc, name), ins=[], outs=[]))
        sch.on("scalar", f)

    gq_init_tgt = [0]

    def emit_init(es_ph):
        ztile = sbt(es_ph, "ztile", [128, D], BF16)

        def g_init(g):
            g.dma_start(ident[:], ci["ident"]).then_inc(SS["d_ini"], 16)
            g.dma_start(ut128[:], ci["ut128"]).then_inc(SS["d_ini"], 16)
            g.dma_start(hmall[:], ci["hmall"]).then_inc(SS["d_ini"], 16)
            g.dma_start(onesc[:], ci["onesc"]).then_inc(SS["d_ini"], 16)
            g.dma_start(onesrb[:], ci["onesr"]).then_inc(SS["d_ini"], 16)
            g.dma_start(cem1[:], ci["cem1"]).then_inc(SS["d_ini"], 16)
            g.dma_start(io32[:], ci["io32"]).then_inc(SS["d_ini"], 16)
            g.memset(ztile[:], 0.0)
            g.drain()
            g.dma_start(dexp_d[PPAD:PTOT, :], ztile[:]).then_inc(SS["d_ini"], 16)
            for zz in range(PTOT // 128):
                g.dma_start(sorted_d[zz * 128:(zz + 1) * 128, :], ztile[:]
                            ).then_inc(SS["d_ini"], 16)
        sch.on("gpsimd", g_init)
        gq0 = sch.inc("d_ini", 16 * (8 + PTOT // 128))
        gq_init_tgt[0] = gq0

        def sy_init(sy):
            sy.dma_start(onesr[:], ci["onesr"]).then_inc(SS["d_ini2"], 16)
            sy.dma_start(tloc[:], ci["tloc"]).then_inc(SS["d_ini2"], 16)
            sy.dma_start(identf[:], ci["ident"]).then_inc(SS["d_ini2"], 16)
            for cc in range(8):
                sy.dma_start(bprec[:, cc:cc + 1],
                             bpre[cc * 128:(cc + 1) * 128][:, None]
                             ).then_inc(SS["d_ini2"], 16)
            sy.dma_start(smA[0:1, 0:1], pi_in[0:1][None, :]).then_inc(SS["d_ini2"], 16)
        sch.on("sync", sy_init)
        sq0 = sch.inc("d_ini2", 16 * 12)

        def v_pi(v, sq0=sq0):
            v.wait_ge(SS["d_ini2"], sq0)
            v.tensor_tensor(smA[0:1, 1:2], smA[0:1, 0:1], smA[0:1, 0:1],
                            op=ALU.mult)
            v.drain()
            v.nop().then_inc(SS["ve"], 1)
        sch.on("vector", v_pi)
        ve0 = sch.inc("ve", 1)

        s_table(LNEXP)

        def s_pi1(s, ve0=ve0):
            s.wait_ge(SS["ve"], ve0)
            s.activation(smA[0:1, 2:3], smA[0:1, 1:2], AF.Ln)
            s.drain()
            s.nop().then_inc(SS["se"], 1)
        sch.on("scalar", s_pi1)
        se_a = sch.inc("se", 1)

        def v_pi2(v, se_a=se_a):
            v.wait_ge(SS["se"], se_a)
            v.tensor_scalar(smA[0:1, 2:3], smA[0:1, 2:3], 0.25, op0=ALU.mult,
                            scalar2=float(-0.5 * np.log(32.0)), op1=ALU.add)
            v.drain()
            v.nop().then_inc(SS["ve"], 1)
        sch.on("vector", v_pi2)
        ve_b = sch.inc("ve", 1)

        def s_pi(s, ve_b=ve_b):
            s.wait_ge(SS["ve"], ve_b)
            s.activation(smA[0:1, 3:4], smA[0:1, 2:3], AF.Exp)
            s.sign(smA[0:1, 4:5], smA[0:1, 0:1])
            s.drain()
            s.nop().then_inc(SS["se"], 1)
        sch.on("scalar", s_pi)
        se0 = sch.inc("se", 1)

        ps = es_ph.enter_context(nc.psum_tensor("p_ini", [128, 512], F32))

        def t_bc(t, se0=se0, gq0=gq0):
            t.wait_ge(SS["se"], se0)
            t.wait_ge(SS["d_ini"], gq0)
            _mm = t.matmul(ps[:, 0:1], onesr[:], smA[0:1, 3:4], start=True, stop=True)
            _mm = t.matmul(ps[:, 1:2], onesr[:], smA[0:1, 4:5], start=True, stop=True)
            _mm.then_inc(SS["pe"], 1)
        sch.on("tensor", t_bc)
        pe0 = sch.inc("pe", 1)

        def v_bc(v, pe0=pe0):
            v.wait_ge(SS["pe"], pe0)
            v.tensor_copy(sprev[:], ps[:, 0:1])
            v.tensor_scalar(sid[:], ident[:], ps[:, 1:2], op0=ALU.mult,
                            scalar2=None)
            v.drain()
            v.nop().then_inc(SS["ve"], 1)
        sch.on("vector", v_bc)
        sch.inc("ve", 1)


    def emit_ln_pass(es_ph, tag, src_dram, dst_dram, src_dtype):
        xb = [sbt(es_ph, f"{tag}x{i}", [128, D], src_dtype) for i in range(2)]
        lnb = [sbt(es_ph, f"{tag}l{i}", [128, D], BF16) for i in range(2)]
        st = [sbt(es_ph, f"{tag}s{i}", [128, 6], F32) for i in range(2)]
        sq_t = sbt(es_ph, f"{tag}q", [128, D], F32)
        for i in range(NTT):
            b = i % 2
            lsem = f"d_xb{b}"
            ssem = f"d_st{b}"

            def sy_ld(sy, i=i, b=b, lsem=lsem, ssem=ssem,
                      w=sch.get(ssem), sw=sch.get("se")):
                sy.wait_ge(SS[ssem], w)
                sy.wait_ge(SS["se"], sw)
                sy.dma_start(xb[b][:], src_dram[i * 128:(i + 1) * 128, :]
                             ).then_inc(SS[lsem], 16)
            sch.on("sync", sy_ld)
            ld_tgt = sch.inc(lsem, 16)

            def v_mean(v, b=b, lsem=lsem, ld_tgt=ld_tgt, sw=sch.get("se")):
                v.wait_ge(SS[lsem], ld_tgt)
                v.wait_ge(SS["se"], sw)
                v.tensor_reduce(st[b][:, 0:1], xb[b][:], op=ALU.add, axis=AX.X)
                v.drain()
                v.tensor_scalar(st[b][:, 1:2], st[b][:, 0:1], 1.0 / D,
                                op0=ALU.mult, scalar2=None)
                v.drain()
                v.nop().then_inc(SS["ve"], 1)
            sch.on("vector", v_mean)
            m_tgt = sch.inc("ve", 1)

            def s_var(s, b=b, m_tgt=m_tgt):
                s.wait_ge(SS["ve"], m_tgt)
                s.activation(sq_t[:], xb[b][:], AF.Square, bias=st[b][:, 1:2],
                             scale=-1.0, accum_out=st[b][:, 2:3])
                s.drain()
                s.nop().then_inc(SS["se"], 1)
            sch.on("scalar", s_var)
            v_tgt = sch.inc("se", 1)

            def v_vr(v, b=b, v_tgt=v_tgt):
                v.wait_ge(SS["se"], v_tgt)
                v.tensor_scalar(st[b][:, 2:3], st[b][:, 2:3], 1.0 / D,
                                op0=ALU.mult, scalar2=LN_EPS, op1=ALU.add)
                v.drain()
                v.nop().then_inc(SS["ve"], 1)
            sch.on("vector", v_vr)
            vr_tgt = sch.inc("ve", 1)

            def s_rs(s, b=b, vr_tgt=vr_tgt):
                s.wait_ge(SS["ve"], vr_tgt)
                s.activation(st[b][:, 3:4], st[b][:, 2:3], AF.Ln)
                s.drain()
                s.activation(st[b][:, 4:5], st[b][:, 3:4], AF.Exp, scale=-0.5)
                s.drain()
                s.nop().then_inc(SS["se"], 1)
            sch.on("scalar", s_rs)
            r_tgt = sch.inc("se", 1)

            def v_norm(v, b=b, ssem=ssem, r_tgt=r_tgt, w=sch.get(ssem)):
                v.wait_ge(SS["se"], r_tgt)
                v.wait_ge(SS[ssem], w)
                v.tensor_scalar(lnb[b][:], xb[b][:], st[b][:, 1:2],
                                op0=ALU.subtract, scalar2=st[b][:, 4:5],
                                op1=ALU.mult)
                v.drain()
                v.nop().then_inc(SS["ve"], 1)
            sch.on("vector", v_norm)
            n_tgt = sch.inc("ve", 1)

            def sy_st(sy, i=i, b=b, ssem=ssem, n_tgt=n_tgt):
                sy.wait_ge(SS["ve"], n_tgt)
                sy.dma_start(dst_dram[i * 128:(i + 1) * 128, :], lnb[b][:]
                             ).then_inc(SS[ssem], 16)
            sch.on("sync", sy_st)
            sch.inc(ssem, 16)

    def emit_p1(es_ph, vtm):
        lnfm = [sbt(es_ph, f"lnfm{c}", [128, N], BF16, is_transpose_dest=True)
                for c in range(8)]
        wbuf = [sbt(es_ph, f"p1w{i}", [128, 4 * D], BF16) for i in range(2)]
        stg = [sbt(es_ph, f"p1stg{i}", [128, 8 * 512], BF16) for i in range(2)]
        sqs = sbt(es_ph, "p1sq", [128, 512], BF16)
        qn2 = sbt(es_ph, "qn2", [16, N], BF16)
        kn2 = sbt(es_ph, "kn2", [16, N], BF16)
        pn2 = sbt(es_ph, "pn2", [1, N], BF16)
        kmax = sbt(es_ph, "kmax", [16, 1], F32)
        pa = es_ph.enter_context(nc.psum_tensor("p1a", [128, 512], F32))
        pb = es_ph.enter_context(nc.psum_tensor("p1b", [128, 512], F32))
        pn = es_ph.enter_context(nc.psum_tensor("p1n", [128, 512], F32))
        pt = es_ph.enter_context(nc.psum_tensor("p1t", [128, 2048], BF16))

        s_table(SILUS)

        def v_ones(v):
            v.memset(qn2[:], 0.0)
            v.memset(kn2[:], 0.0)
            v.memset(pn2[:], 0.0)
            v.memset(vtm.rearrange("p (t h o) -> p t h o", t=NJT, h=H, o=65)
                     [:, :, :, 64:65], 1.0)
            v.drain()
            v.nop().then_inc(SS["ve"], 1)
        sch.on("vector", v_ones)
        sch.inc("ve", 1)

        def s_trl(s):
            for c in range(8):
                s.dma_start(lnfm[c][:], ln1_d[:, c * 128:(c + 1) * 128],
                            transpose=True).then_inc(SS["d_trl"], 16)
        sch.on("scalar", s_trl)
        aq_ln = sch.inc("d_trl", 16 * 8)

        weights = [("k", Wk), ("v", Wv), ("q", Wq), ("pre", Wpre)][:P1_NW]
        w_rdy = {}

        def emit_load_w(widx, half, wait_pe):
            def g_ld(g, widx=widx, half=half, wait_pe=wait_pe):
                g.wait_ge(SS["pe"], wait_pe)
                nm, ap = weights[widx]
                g.dma_start(
                    wbuf[half].rearrange("p (c d) -> p c d", c=4),
                    ap.rearrange("(c p) d -> p c d", p=128)[:, half * 4:
                                                           half * 4 + 4, :]
                            ).then_inc(SS[f"d_w{half}"], 16)
            sch.on("gpsimd", g_ld)
            return sch.inc(f"d_w{half}", 16)

        w_rdy[(0, 0)] = emit_load_w(0, 0, 0)
        w_rdy[(0, 1)] = emit_load_w(0, 1, 0)
        NW = len(weights)
        evac_done = {}

        if P1_PART == "skel":
            weights2 = []
        else:
            weights2 = list(enumerate(weights))
        for widx, (nm, wap) in weights2:
            for tk in range(8):
                sg = stg[tk % 2]
                sgsem = f"d_sg{tk % 2}"
                stored = sch.get(sgsem)
                normed = sch.get("se")
                for co in range(8):
                    ps = pa if co % 2 == 0 else pb
                    pkey = id(ps)

                    def t_mm(t, widx=widx, tk=tk, co=co, ps=ps,
                             w0=w_rdy[(widx, 0)], w1=w_rdy[(widx, 1)],
                             ev_w=evac_done.get(id(ps), (0, 0))):
                        if tk == 0 and co == 0:
                            t.wait_ge(SS["d_w0"], w0)
                            t.wait_ge(SS["d_w1"], w1)
                            if widx == 0:
                                t.wait_ge(SS["d_trl"], aq_ln)
                        t.wait_ge(SS["ve"], ev_w[0])
                        t.wait_ge(SS["se"], ev_w[1])
                        for cin in range(8):
                            _mm = t.matmul(ps[:],
                                     wbuf[cin // 4][:, (cin % 4) * D + co * 128:
                                                    (cin % 4) * D + (co + 1) * 128],
                                     lnfm[cin][:, tk * 512:(tk + 1) * 512],
                                     start=(cin == 0), stop=(cin == 7))
                        _mm.then_inc(SS["pe"], 1)
                    sch.on("tensor", t_mm)
                    mm_tgt = sch.inc("pe", 1)

                    if nm == "pre":
                        def s_sil(s, co=co, ps=ps, mm_tgt=mm_tgt, sg=sg,
                                  sgsem=sgsem, stw=stored):
                            s.wait_ge(SS["pe"], mm_tgt)
                            s.wait_ge(SS[sgsem], stw)
                            s.activation(sg[:, co * 512:(co + 1) * 512], ps[:],
                                         AF.Silu, bias=bprec[:, co:co + 1])
                            s.drain()
                            s.nop().then_inc(SS["se"], 1)
                        sch.on("scalar", s_sil)
                        sil_tgt = sch.inc("se", 1)

                        def v_sc(v, co=co, sil_tgt=sil_tgt, sg=sg):
                            v.wait_ge(SS["se"], sil_tgt)
                            v.tensor_scalar(sg[:, co * 512:(co + 1) * 512],
                                            sg[:, co * 512:(co + 1) * 512],
                                            sprev[:, 0:1], op0=ALU.mult,
                                            scalar2=None)
                            v.drain()
                            v.nop().then_inc(SS["ve"], 1)
                        sch.on("vector", v_sc)
                        sch.inc("ve", 1)
                        evac_done[pkey] = (sch.get("ve"), sch.get("se"))
                    else:
                        scale = 0.125 if nm == "q" else 1.0

                        def v_ev(v, co=co, ps=ps, mm_tgt=mm_tgt, scale=scale,
                                 sg=sg, sgsem=sgsem, stw=stored, nw=normed):
                            v.wait_ge(SS["pe"], mm_tgt)
                            v.wait_ge(SS[sgsem], stw)
                            v.wait_ge(SS["se"], nw)
                            v.tensor_scalar(sg[:, co * 512:(co + 1) * 512], ps[:],
                                            scale, op0=ALU.mult, scalar2=None)
                            v.drain()
                            v.nop().then_inc(SS["ve"], 1)
                        sch.on("vector", v_ev)
                        sch.inc("ve", 1)
                        evac_done[pkey] = (sch.get("ve"), sch.get("se"))
                evac_ve = sch.get("ve")
                evac_se = sch.get("se")

                if P1_PART == "mm":
                    continue
                if nm != "v":
                    ddst = {"k": kt_d, "q": qa_d, "pre": pre_d}[nm]

                    def sy_st(sy, tk=tk, ddst=ddst, ve_w=evac_ve, se_w=evac_se,
                              sg=sg, sgsem=sgsem):
                        sy.wait_ge(SS["ve"], ve_w)
                        sy.wait_ge(SS["se"], se_w)
                        sy.dma_start(
                            ddst.rearrange("(c p) n -> p c n", p=128)
                            [:, :, tk * 512:(tk + 1) * 512],
                            sg.rearrange("p (c n) -> p c n", c=8)
                                    ).then_inc(SS[sgsem], 16)
                    sch.on("sync", sy_st)
                    sch.inc(sgsem, 16)
                else:
                    for co in range(8):
                        for j in range(4):
                            jt = tk * 4 + j

                            def t_tr(t, co=co, j=j, ve_w=evac_ve, sg=sg,
                                     evv=max(0, sch.get("ve") - 1)):
                                t.wait_ge(SS["ve"], ve_w)
                                t.wait_ge(SS["ve"], evv)
                                _mm = t.transpose(
                                    pt[:, (j % 2) * 1024:(j % 2) * 1024 + 128],
                                    sg[:, co * 512 + j * 128:
                                       co * 512 + (j + 1) * 128],
                                    ident[:])
                                _mm.then_inc(SS["pe"], 1)
                            sch.on("tensor", t_tr)
                            tr_tgt = sch.inc("pe", 1)

                            def v_tr(v, co=co, jt=jt, j=j, tr_tgt=tr_tgt):
                                v.wait_ge(SS["pe"], tr_tgt)
                                v.tensor_copy(
                                    vtm.rearrange("p (t h o) -> p t h o",
                                                  t=NJT, h=H, o=65)
                                    [:, jt, 2 * co:2 * co + 2, 0:64],
                                    pt[:, (j % 2) * 1024:(j % 2) * 1024 + 128]
                                    .rearrange("p (a b) -> p a b", a=2))
                                v.drain()
                                v.nop().then_inc(SS["ve"], 1)
                            sch.on("vector", v_tr)
                            sch.inc("ve", 1)

                if nm in ("q", "k", "pre") and P1_PART not in ("mm", "mmst"):
                    ndst = {"q": qn2, "k": kn2, "pre": pn2}[nm]
                    for co in range(8):
                        def s_sq(s, co=co, ve_w=evac_ve, se_w=evac_se, sg=sg,
                                 pw=sch.get("pe")):
                            s.wait_ge(SS["ve"], ve_w)
                            s.wait_ge(SS["pe"], pw)
                            s.activation(sqs[:], sg[:, co * 512:(co + 1) * 512],
                                         AF.Square)
                            s.drain()
                            s.nop().then_inc(SS["se"], 1)
                        sch.on("scalar", s_sq)
                        sq_tgt = sch.inc("se", 1)

                        def t_nm(t, nm=nm, co=co, sq_tgt=sq_tgt,
                                 vw=sch.get("ve")):
                            t.wait_ge(SS["se"], sq_tgt)
                            if co == 0:
                                t.wait_ge(SS["ve"], vw)
                            lhs = (onesc[:] if nm == "pre"
                                   else hmall[:, co * 16:(co + 1) * 16])
                            rows = 1 if nm == "pre" else 16
                            _mm = t.matmul(pn[0:rows, 0:512], lhs, sqs[:],
                                     start=(co == 0), stop=(co == 7))
                            _mm.then_inc(SS["pe"], 1)
                        sch.on("tensor", t_nm)
                        sch.inc("pe", 1)

                    def v_nm(v, nm=nm, tk=tk, ndst=ndst, pw=sch.get("pe")):
                        v.wait_ge(SS["pe"], pw)
                        rows = 1 if nm == "pre" else 16
                        v.tensor_copy(ndst[0:rows, tk * 512:(tk + 1) * 512],
                                      pn[0:rows, 0:512])
                        v.drain()
                        v.nop().then_inc(SS["ve"], 1)
                    sch.on("vector", v_nm)
                    sch.inc("ve", 1)
            if widx + 1 < NW:
                w_rdy[(widx + 1, 0)] = emit_load_w(widx + 1, 0, sch.get("pe"))
                w_rdy[(widx + 1, 1)] = emit_load_w(widx + 1, 1, sch.get("pe"))

        if P1_PART in ("skel", "loop"):
            return
        s_table(LNEXP)

        def s_post(s, vw=sch.get("ve")):
            s.wait_ge(SS["ve"], vw)
            s.activation(qn2[:], qn2[:], AF.Ln)
            s.drain()
            s.activation(qn2[:], qn2[:], AF.Exp, scale=0.5)
            s.drain()
            s.activation(kn2[:], kn2[:], AF.Ln)
            s.drain()
            s.activation(kn2[:], kn2[:], AF.Exp, scale=0.5)
            s.activation(pn2[:], pn2[:], AF.Ln)
            s.drain()
            s.activation(pn2[:], pn2[:], AF.Exp, scale=0.5)
            s.drain()
            s.nop().then_inc(SS["se"], 1)
        sch.on("scalar", s_post)
        sq2 = sch.inc("se", 1)

        def v_post(v, sq2=sq2):
            v.wait_ge(SS["se"], sq2)
            v.tensor_reduce(kmax[:], kn2[:], op=ALU.max, axis=AX.X)
            v.tensor_reduce(smA[0:1, 5:6], pn2[:], op=ALU.max, axis=AX.X)
            v.drain()
            v.tensor_scalar(pn2[:], pn2[:], smA[0:1, 5:6], op0=ALU.mult,
                            scalar2=None)
            v.drain()
            v.nop().then_inc(SS["ve"], 1)
        sch.on("vector", v_post)
        pn_tgt = sch.inc("ve", 1)

        for tk in range(8):
            def t_pb(t, tk=tk, pn_tgt=pn_tgt, vw=sch.get("ve")):
                t.wait_ge(SS["ve"], pn_tgt)
                t.wait_ge(SS["ve"], vw)
                _mm = t.matmul(pn[0:16, 0:512], onesrb[0:1, 0:16],
                         pn2[:, tk * 512:(tk + 1) * 512], start=True, stop=True)
                _mm.then_inc(SS["pe"], 1)
            sch.on("tensor", t_pb)
            tgt = sch.inc("pe", 1)

            def v_pb(v, tk=tk, tgt=tgt):
                v.wait_ge(SS["pe"], tgt)
                v.scalar_tensor_tensor(uall[:, tk * 512:(tk + 1) * 512],
                                       qn2[:, tk * 512:(tk + 1) * 512], kmax[:],
                                       pn[0:16, 0:512], op0=ALU.mult,
                                       op1=ALU.add)
                v.drain()
                v.nop().then_inc(SS["ve"], 1)
            sch.on("vector", v_pb)
            sch.inc("ve", 1)

    def emit_p2(es_ph, vtm):
        ktc = [sbt(es_ph, f"kt{c}", [128, N], BF16) for c in range(8)]
        qch = sbt(es_ph, "qch", [128, 8 * IC], BF16)
        strip = sbt(es_ph, "strip", [128, NJT * IC], BF16)
        prej = [sbt(es_ph, f"prej{i}", [128, 8 * 256], BF16) for i in range(2)]
        premov = qch
        ptile = [sbt(es_ph, f"ptile{i}", [128, 2048], BF16) for i in range(2)]
        bneg = sbt(es_ph, "bneg", [128, 16], F32)
        bprojb = sbt(es_ph, "bprojb", [128, D], F32)
        scrF = sbt(es_ph, "scrF", [128, D], F32)
        o65s = [scrF[:, 0:512], scrF[:, 512:1024]]
        x3 = scrF
        uasm_sc = scrF[:, 0:33]
        rcp = sbt(es_ph, "rcp", [128, 8], F32)
        otm = sbt(es_ph, "otm", [128, 2 * 1024], BF16)
        ofm = sbt(es_ph, "ofm", [128, 8 * IC], BF16)
        wpj = sbt(es_ph, "wpj", [128, 8 * 1024], BF16)
        pjs = otm
        ln3 = qch[:, 0:D]
        xm3 = qch[:, D:2 * D]

        def sy_bp(sy):
            sy.dma_start(bprojb[0:1, :], bproj[None, :]).then_inc(SS["d_x"], 16)
        sch.on("sync", sy_bp)
        bp_tgt = sch.inc("d_x", 16)

        with nc.psum_tensor("p2ub", [128, 48], F32) as pub:
            def v_ub(v, vw=sch.get("ve")):
                v.wait_ge(SS["ve"], vw)
                v.tensor_reduce(uasm_sc[0:16, 0:16],
                                uall.rearrange("h (c i) -> h c i", c=NCH),
                                op=ALU.max, axis=AX.X)
                v.drain()
                v.nop().then_inc(SS["ve"], 1)
            sch.on("vector", v_ub)
            ub1 = sch.inc("ve", 1)

            def t_ub(t, ub1=ub1):
                t.wait_ge(SS["ve"], ub1)
                _mm = t.transpose(pub[0:16, 0:16], uasm_sc[0:16, 0:16],
                                  identf[0:16, 0:16])
                _mm.then_inc(SS["pe"], 1)
            sch.on("tensor", t_ub)
            ub2 = sch.inc("pe", 1)

            def v_ub2(v, ub2=ub2):
                v.wait_ge(SS["pe"], ub2)
                v.tensor_reduce(uasm_sc[0:16, 16:17], pub[0:16, 0:16],
                                op=ALU.max, axis=AX.X, negate=True)
                v.drain()
                v.nop().then_inc(SS["ve"], 1)
            sch.on("vector", v_ub2)
            ub3 = sch.inc("ve", 1)

            def t_ub2(t, ub3=ub3):
                t.wait_ge(SS["ve"], ub3)
                _mm = t.transpose(pub[0:1, 16:32], uasm_sc[0:16, 16:17],
                                  identf[0:16, 0:16])
                _mm.then_inc(SS["pe"], 1)
            sch.on("tensor", t_ub2)
            ub4 = sch.inc("pe", 1)

            def v_ub3(v, ub4=ub4):
                v.wait_ge(SS["pe"], ub4)
                v.tensor_copy(uasm_sc[0:1, 17:33], pub[0:1, 16:32])
                v.drain()
                v.nop().then_inc(SS["ve"], 1)
            sch.on("vector", v_ub3)
            ub5 = sch.inc("ve", 1)

            def t_ub3(t, ub5=ub5):
                t.wait_ge(SS["ve"], ub5)
                _mm = t.matmul(pub[:, 32:48], onesr[:], uasm_sc[0:1, 17:33],
                               start=True, stop=True)
                _mm.then_inc(SS["pe"], 1)
            sch.on("tensor", t_ub3)
            ub6 = sch.inc("pe", 1)

            def v_ub4(v, ub6=ub6):
                v.wait_ge(SS["pe"], ub6)
                v.tensor_copy(bneg[:], pub[:, 32:48])
                v.drain()
                v.nop().then_inc(SS["ve"], 1)
            sch.on("vector", v_ub4)
            sch.inc("ve", 1)

        def g_wpj(g):
            g.dma_start(wpj.rearrange("p (c d) -> p c d", c=8),
                        Wproj.rearrange("(c p) d -> p c d", p=128)
                        ).then_inc(SS["d_wp"], 16)
        sch.on("gpsimd", g_wpj)
        wpj_tgt = sch.inc("d_wp", 16)

        def sy_kt(sy):
            for c in range(8):
                sy.dma_start(ktc[c][:], kt_d[c * 128:(c + 1) * 128, :]
                        ).then_inc(SS["d_kt"], 16)
        sch.on("sync", sy_kt)
        kt_tgt = sch.inc("d_kt", 16 * 8)

        with nc.psum_tensor("p2bp", [128, 512], F32) as pbp:
            for half in range(2):
                def t_bp(t, half=half, vw=sch.get("ve"), bp_tgt=bp_tgt):
                    t.wait_ge(SS["ve"], vw)
                    if half == 0:
                        t.wait_ge(SS["d_x"], bp_tgt)
                    for j in range(4):
                        cc = half * 4 + j
                        _mm = t.matmul(pbp[:, j * 128:(j + 1) * 128], onesr[:],
                                 bprojb[0:1, cc * 128:(cc + 1) * 128],
                                 start=True, stop=True)
                    _mm.then_inc(SS["pe"], 1)
                sch.on("tensor", t_bp)
                pet = sch.inc("pe", 1)

                def v_bp(v, half=half, pet=pet):
                    v.wait_ge(SS["pe"], pet)
                    v.tensor_copy(bprojb[:, half * 512:(half + 1) * 512], pbp[:])
                    v.drain()
                    v.nop().then_inc(SS["ve"], 1)
                sch.on("vector", v_bp)
                sch.inc("ve", 1)

        pj_ld_hist = {}
        vfm_hist = {}
        for ch in range(NCH):
            i0 = ch * IC

            def sy_q(sy, ch=ch, i0=i0, pw=sch.get("pe")):
                sy.wait_ge(SS["pe"], pw)
                sy.dma_start(premov.rearrange("p (c n) -> p c n", c=8),
                             pre_d.rearrange("(c p) n -> p c n", p=128)
                             [:, :, i0:i0 + IC]
                            ).then_inc(SS["d_q"], 16)
            sch.on("sync", sy_q)
            q_tgt = sch.inc("d_q", 16)

            with nc.psum_tensor(f"p2sA{ch}", [128, 512], F32) as pstA, \
                 nc.psum_tensor(f"p2sB{ch}", [128, 512], F32) as pstB:
                pst2 = [pstA, pstB]
                ev_hist = {}
                st_hist = {}
                ch_ve0 = vfm_hist.get(ch - 1, sch.get("ve") if ch == 0 else 0)
                if ch > 0 and ch_ve0 == 0:
                    ch_ve0 = sch.get("ve")
                for ld in range(16):
                    pj = prej[ld % 2]
                    pjsem = f"d_pj{ld % 2}"
                    qeng = "sync" if ld % 2 == 0 else "gpsimd"
                    ldw = st_hist.get((ch, ld - 2), pj_ld_hist.get(ch - 1, 0)
                                      if ld < 2 else 0)

                    def q_pj(e, ld=ld, pj=pj, pjsem=pjsem, ldw=ldw):
                        e.wait_ge(SS["pe"], ldw)
                        e.dma_start(pj.rearrange("p (c n) -> p c n", c=8),
                                    pre_d.rearrange("(c p) n -> p c n", p=128)
                                    [:, :, ld * 256:(ld + 1) * 256]
                                    ).then_inc(SS[pjsem], 16)
                    sch.on(qeng, q_pj)
                    pj_tgt = sch.inc(pjsem, 16)

                    for sub in range(2):
                        jt = ld * 2 + sub
                        pst = pst2[jt % 2]

                        def t_st(t, sub=sub, pj=pj, pjsem=pjsem,
                                 pj_tgt=pj_tgt, q_tgt=q_tgt, pst=pst,
                                 vw=ev_hist.get(jt - 2, ch_ve0)):
                            t.wait_ge(SS[pjsem], pj_tgt)
                            t.wait_ge(SS["d_q"], q_tgt)
                            t.wait_ge(SS["d_kt"], kt_tgt)
                            t.wait_ge(SS["ve"], vw)
                            for cc in range(8):
                                _mm = t.matmul(
                                    pst[:, 0:IC],
                                    pj[:, cc * 256 + sub * 128:
                                       cc * 256 + (sub + 1) * 128],
                                    premov[:, cc * IC:(cc + 1) * IC],
                                    start=(cc == 0), stop=(cc == 7))
                            _mm.then_inc(SS["pe"], 1)
                        sch.on("tensor", t_st)
                        smm = sch.inc("pe", 1)
                        st_hist[(ch, ld)] = smm

                        def v_st(v, jt=jt, smm=smm, pst=pst):
                            v.wait_ge(SS["pe"], smm)
                            v.tensor_copy(strip[:, jt * IC:(jt + 1) * IC],
                                          pst[:, 0:IC])
                            v.drain()
                            v.nop().then_inc(SS["ve"], 1)
                        sch.on("vector", v_st)
                        ev_hist[jt] = sch.inc("ve", 1)
                pj_ld_hist[ch] = sch.get("pe")

            def sy_q2(sy, ch=ch, i0=i0, pw=sch.get("pe")):
                sy.wait_ge(SS["pe"], pw)
                sy.dma_start(qch.rearrange("p (c n) -> p c n", c=8),
                             qa_d.rearrange("(c p) n -> p c n", p=128)
                             [:, :, i0:i0 + IC]).then_inc(SS["d_q"], 16)
            sch.on("sync", sy_q2)
            q2_tgt = sch.inc("d_q", 16)


            strip_rdy = sch.get("ve")
            for sg in range(2):
                sg_ve = sch.get("ve")
                es_ob = ExitStack()
                es_sc = ExitStack()
                ob = [es_ob.enter_context(
                    nc.psum_tensor(f"ob{i}_{ch}_{sg}", [128, 512], F32))
                    for i in range(4)]
                scq = [es_sc.enter_context(
                    nc.psum_tensor(f"sc{q}_{ch}_{sg}", [128, 1024], F32))
                    for q in range(2)]
                exp_tgt = [[None] * NJT for _ in range(2)]
                av_tgt = [[None] * NJT for _ in range(2)]

                def mk_sc(jt, qd, sw, vw0, sg=sg, scq=scq, q2=q2_tgt):
                    def t_sc(t, jt=jt, qd=qd, sw=sw, vw0=vw0, sg=sg, scq=scq,
                             q2=q2):
                        if sw is not None:
                            t.wait_ge(SS["se"], sw)
                        if vw0 is not None:
                            t.wait_ge(SS["ve"], vw0)
                            t.wait_ge(SS["d_q"], q2)
                        st_ap = strip[:, jt * IC:(jt + 1) * IC]
                        h0 = sg * 8 + qd * 4
                        for r in range(4):
                            _mm = t.matmul(
                                scq[qd][:, r * IC:(r + 1) * IC], sid[:],
                                st_ap, start=(r % 2 == 0), stop=False,
                                skip_group_check=True)
                        order = [(h0, 0), (h0 + 1, 2), (h0 + 2, 1),
                                 (h0 + 3, 3)]
                        for hh, reg in order:
                            cc = hh // 2
                            hb = (hh % 2) * 64
                            _mm = t.matmul(
                                scq[qd][:, reg * IC:(reg + 1) * IC],
                                ktc[cc][hb:hb + 64,
                                        jt * 128:(jt + 1) * 128],
                                qch[hb:hb + 64, cc * IC:(cc + 1) * IC],
                                start=False, stop=(reg >= 2),
                                skip_group_check=True)
                        _mm.then_inc(SS["pe"], 1)
                    sch.on("tensor", t_sc)
                    return sch.inc("pe", 1)

                def mk_ex(jt, qd, sc_t, av_w, scq=scq):
                    def s_ex(s, jt=jt, qd=qd, sc_t=sc_t, av_w=av_w, scq=scq):
                        s.wait_ge(SS["pe"], sc_t)
                        if av_w is not None:
                            s.wait_ge(SS["pe"], av_w)
                        s.activation(ptile[jt % 2][:, qd * 1024:(qd + 1) * 1024],
                                     scq[qd][:], AF.Exp, bias=bneg[:, ch:ch + 1])
                        s.drain()
                        s.nop().then_inc(SS["se"], 1)
                    sch.on("scalar", s_ex)
                    return sch.inc("se", 1)

                def mk_av(jt, qd, ex_t, sg=sg, ob=ob):
                    def t_av(t, jt=jt, qd=qd, ex_t=ex_t, sg=sg, ob=ob):
                        t.wait_ge(SS["se"], ex_t)
                        h0 = sg * 8 + qd * 4
                        order = [(h0, 0), (h0 + 2, 1), (h0 + 1, 2),
                                 (h0 + 3, 3)]
                        for hh, reg in order:
                            obk = ob[qd * 2 + (0 if reg < 2 else 1)]
                            col = reg % 2
                            _mm = t.matmul(
                                obk[0:65, col * IC:(col + 1) * IC],
                                vtm.rearrange("p (t h o) -> p t h o",
                                              t=NJT, h=H, o=65)[:, jt, hh, :],
                                ptile[jt % 2][:, qd * 1024 + reg * IC:
                                              qd * 1024 + (reg + 1) * IC],
                                start=(jt == 0 and col == 0),
                                stop=(jt == NJT - 1),
                                skip_group_check=True)
                        _mm.then_inc(SS["pe"], 1)
                    sch.on("tensor", t_av)
                    return sch.inc("pe", 1)

                for jt in range(NJT):
                    for qd in range(2):
                        sw = exp_tgt[qd][jt - 1] if jt >= 1 else None
                        vw0 = max(strip_rdy, sg_ve) if jt == 0 else None
                        sc_t = mk_sc(jt, qd, sw, vw0)
                        av_w = av_tgt[qd][jt - 2] if jt >= 2 else None
                        exp_tgt[qd][jt] = mk_ex(jt, qd, sc_t, av_w)
                    if jt >= 1:
                        for qd in range(2):
                            av_tgt[qd][jt - 1] = mk_av(jt - 1, qd,
                                                       exp_tgt[qd][jt - 1])
                for qd in range(2):
                    av_tgt[qd][NJT - 1] = mk_av(NJT - 1, qd,
                                                exp_tgt[qd][NJT - 1])

                es_sc.close()
                attnv_done = sch.get("pe")
                with nc.psum_tensor(f"ptrA_{ch}_{sg}", [128, 512], F32) as ptrA, \
                     nc.psum_tensor(f"ptrB_{ch}_{sg}", [128, 512], F32) as ptrB:
                    ptr2 = [ptrA, ptrB]
                    oe_hist = {}
                    on_hist = {}
                    for ob_i in range(4):
                        qd = ob_i // 2
                        h0 = sg * 8 + qd * 4
                        heads = ((h0, h0 + 2) if ob_i % 2 == 0
                                 else (h0 + 1, h0 + 3))
                        o65 = o65s[ob_i % 2]
                        ptr = ptr2[ob_i % 2]

                        def s_oe(s, obk=ob[ob_i], o65=o65,
                                 attnv_done=attnv_done,
                                 pw=oe_hist.get(ob_i - 2, 0),
                                 vw=sch.get("ve") if ob_i < 2 else 0):
                            s.wait_ge(SS["pe"], attnv_done)
                            if pw:
                                s.wait_ge(SS["pe"], pw)
                            if vw:
                                s.wait_ge(SS["ve"], vw)
                            s.copy(o65[0:65, :], obk[0:65, :])
                            s.drain()
                            s.nop().then_inc(SS["se"], 1)
                        sch.on("scalar", s_oe)
                        cp_tgt = sch.inc("se", 1)

                        def t_otr(t, cp_tgt=cp_tgt, ptr=ptr, o65=o65,
                                  vw=on_hist.get(ob_i - 2, 0)):
                            t.wait_ge(SS["se"], cp_tgt)
                            if vw:
                                t.wait_ge(SS["ve"], vw)
                            for bb in range(4):
                                _mm = t.transpose(ptr[:, bb * 65:(bb + 1) * 65],
                                                  o65[0:65,
                                                      bb * 128:(bb + 1) * 128],
                                                  identf[0:65, 0:65])
                            _mm.then_inc(SS["pe"], 1)
                        sch.on("tensor", t_otr)
                        oe_hist[ob_i] = tr_tgt = sch.inc("pe", 1)

                        def v_on(v, heads=heads, tr_tgt=tr_tgt, ptr=ptr,
                                 ob_i=ob_i):
                            v.wait_ge(SS["pe"], tr_tgt)
                            for bb in range(4):
                                v.reciprocal(rcp[:, (ob_i % 2) * 4 + bb:
                                                 (ob_i % 2) * 4 + bb + 1],
                                             ptr[:, bb * 65 + 64:bb * 65 + 65])
                            v.drain()
                            for bb in range(4):
                                hh = heads[bb // 2]
                                ih = bb % 2
                                v.tensor_scalar(
                                    otm[:, ih * 1024 + hh * 64:
                                        ih * 1024 + (hh + 1) * 64],
                                    ptr[:, bb * 65:bb * 65 + 64],
                                    rcp[:, (ob_i % 2) * 4 + bb:
                                        (ob_i % 2) * 4 + bb + 1],
                                    op0=ALU.mult, scalar2=None)
                            v.drain()
                            v.nop().then_inc(SS["ve"], 1)
                        sch.on("vector", v_on)
                        on_hist[ob_i] = sch.inc("ve", 1)
                es_ob.close()

            with nc.psum_tensor(f"pfm{ch}", [128, 2048], BF16) as pfm, \
                 nc.psum_tensor(f"ppjA{ch}", [128, 512], F32) as ppjA, \
                 nc.psum_tensor(f"ppjB{ch}", [128, 512], F32) as ppjB, \
                 nc.psum_tensor(f"ptm{ch}", [128, 1024], BF16) as ptm:
                ppj2 = [ppjA, ppjB]
                pj_hist = {}
                for ih in range(2):
                    for cc in range(8):
                        def t_fm(t, ih=ih, cc=cc, pfm=pfm, vw=sch.get("ve"),
                                 evv=max(0, sch.get("ve") - 1)):
                            t.wait_ge(SS["ve"], vw if (ih == 0 and cc == 0)
                                      else evv)
                            _mm = t.transpose(
                                pfm[:, (cc % 2) * 1024:(cc % 2) * 1024 + 128],
                                otm[:, ih * 1024 + cc * 128:
                                    ih * 1024 + (cc + 1) * 128],
                                ident[:])
                            _mm.then_inc(SS["pe"], 1)
                        sch.on("tensor", t_fm)
                        trt = sch.inc("pe", 1)

                        def v_fm(v, ih=ih, cc=cc, trt=trt, pfm=pfm):
                            v.wait_ge(SS["pe"], trt)
                            v.tensor_copy(
                                ofm[:, cc * IC + ih * 128:
                                    cc * IC + (ih + 1) * 128],
                                pfm[:, (cc % 2) * 1024:(cc % 2) * 1024 + 128])
                            v.drain()
                            v.nop().then_inc(SS["ve"], 1)
                        sch.on("vector", v_fm)
                        vfm_hist[ch] = sch.inc("ve", 1)

                for co in range(8):
                    ppj = ppj2[co % 2]

                    def t_pj(t, co=co, ppj=ppj, wt=wpj_tgt,
                             vw=sch.get("ve"),
                             evw=pj_hist.get(co - 2, 0)):
                        if co == 0:
                            t.wait_ge(SS["d_wp"], wt)
                        t.wait_ge(SS["ve"], vw if co == 0 else evw)
                        for cin in range(8):
                            _mm = t.matmul(
                                ppj[:, 0:IC],
                                wpj[:, cin * 1024 + co * 128:
                                    cin * 1024 + (co + 1) * 128],
                                ofm[:, cin * IC:(cin + 1) * IC],
                                start=(cin == 0), stop=(cin == 7))
                        _mm.then_inc(SS["pe"], 1)
                    sch.on("tensor", t_pj)
                    pj_tgt = sch.inc("pe", 1)

                    def v_pj(v, co=co, pj_tgt=pj_tgt, ppj=ppj):
                        v.wait_ge(SS["pe"], pj_tgt)
                        v.tensor_copy(pjs[:, co * IC:(co + 1) * IC], ppj[:, 0:IC])
                        v.drain()
                        v.nop().then_inc(SS["ve"], 1)
                    sch.on("vector", v_pj)
                    pj_hist[co] = sch.inc("ve", 1)

                for ih in range(2):
                    def sy_x(sy, i0=i0, ih=ih, vw=sch.get("ve")):
                        sy.wait_ge(SS["ve"], vw)
                        sy.dma_start(x3[:],
                                     x_in[i0 + ih * 128:i0 + (ih + 1) * 128, :]
                                     ).then_inc(SS["d_x"], 16)
                        sy.dma_start(ln3[:],
                                     ln1_d[i0 + ih * 128:i0 + (ih + 1) * 128, :]
                                    ).then_inc(SS["d_x"], 16)
                    sch.on("sync", sy_x)
                    x_tgt = sch.inc("d_x", 32)

                    def t_tm(t, ih=ih, ptm=ptm, vw=sch.get("ve")):
                        t.wait_ge(SS["ve"], vw)
                        for co in range(8):
                            _mm = t.transpose(ptm[:, co * 128:(co + 1) * 128],
                                        pjs[:, co * IC + ih * 128:
                                            co * IC + (ih + 1) * 128],
                                        ident[:])
                        _mm.then_inc(SS["pe"], 1)
                    sch.on("tensor", t_tm)
                    tmt = sch.inc("pe", 1)

                    def v_cmb(v, ih=ih, tmt=tmt, x_tgt=x_tgt, ptm=ptm,
                              xmw=sch.get("d_xm")):
                        v.wait_ge(SS["pe"], tmt)
                        v.wait_ge(SS["d_x"], x_tgt)
                        v.wait_ge(SS["d_xm"], xmw)
                        v.scalar_tensor_tensor(xm3[:], ptm[:], 1.0, bprojb[:],
                                               op0=ALU.mult, op1=ALU.add)
                        v.drain()
                        v.tensor_tensor(xm3[:], xm3[:], ln3[:], op=ALU.add)
                        v.drain()
                        v.tensor_tensor(xm3[:], xm3[:], x3[:], op=ALU.add)
                        v.drain()
                        v.nop().then_inc(SS["ve"], 1)
                    sch.on("vector", v_cmb)
                    cm_tgt = sch.inc("ve", 1)

                    def sy_xm(sy, i0=i0, ih=ih, cm_tgt=cm_tgt):
                        sy.wait_ge(SS["ve"], cm_tgt)
                        sy.dma_start(
                            xmid_d[i0 + ih * 128:i0 + (ih + 1) * 128, :], xm3[:]
                                    ).then_inc(SS["d_xm"], 16)
                    sch.on("sync", sy_xm)
                    sch.inc("d_xm", 16)

    def emit_f0(es_ph, ln2fm):
        ggw = sbt(es_ph, "ggw", [128, 8 * 32], BF16)
        lgfm = sbt(es_ph, "lgfm", [32, N], F32)
        lg_t = sbt(es_ph, "lg_t", [128, 32], F32)
        mx = sbt(es_ph, "mx", [128, 8], F32)
        ohf = sbt(es_ph, "ohf", [128, 32], F32)
        ohb = sbt(es_ph, "ohb", [128, 32], BF16)
        wnf = sbt(es_ph, "wnf", [128, 32], F32)
        cntc = sbt(es_ph, "cntc", [32, 1], F32)
        runrow = sbt(es_ph, "runrow", [1, 32], F32)
        bcr = sbt(es_ph, "bcr", [128, 32], F32)
        tmp32 = sbt(es_ph, "tmp32", [128, 32], F32)
        pv = sbt(es_ph, "pv", [128, 8], F32)
        ln2t = sbt(es_ph, "ln2t", [128, D], BF16)
        xst = sbt(es_ph, "xst", [128, D], BF16)

        def s_trl(s):
            for c in range(8):
                s.dma_start(ln2fm[c][:], ln2_d[:, c * 128:(c + 1) * 128],
                            transpose=True).then_inc(SS["d_trl"], 16)
        sch.on("scalar", s_trl)
        aq_ln2 = sch.inc("d_trl", 16 * 8)

        def g_gw(g):
            g.dma_start(ggw.rearrange("p (c e) -> p c e", c=8),
                        gate_w.rearrange("(c p) e -> p c e", p=128)
                        ).then_inc(SS["d_wp"], 16)
        sch.on("gpsimd", g_gw)
        gw_tgt = sch.inc("d_wp", 16)

        plg = es_ph.enter_context(nc.psum_tensor("plg", [32, 512], F32))
        prt = es_ph.enter_context(nc.psum_tensor("prt", [128, 256], F32))
        for tk in range(8):
            def t_lg(t, tk=tk, gw_tgt=gw_tgt, aq_ln2=aq_ln2,
                     vw=sch.get("ve")):
                if tk == 0:
                    t.wait_ge(SS["d_wp"], gw_tgt)
                    t.wait_ge(SS["d_trl"], aq_ln2)
                t.wait_ge(SS["ve"], vw)
                for cin in range(8):
                    _mm = t.matmul(plg[:], ggw[:, cin * 32:(cin + 1) * 32],
                             ln2fm[cin][:, tk * 512:(tk + 1) * 512],
                             start=(cin == 0), stop=(cin == 7))
                _mm.then_inc(SS["pe"], 1)
            sch.on("tensor", t_lg)
            lg_tgt = sch.inc("pe", 1)

            def v_lg(v, tk=tk, lg_tgt=lg_tgt):
                v.wait_ge(SS["pe"], lg_tgt)
                v.tensor_copy(lgfm[:, tk * 512:(tk + 1) * 512], plg[:])
                v.drain()
                v.nop().then_inc(SS["ve"], 1)
            sch.on("vector", v_lg)
            sch.inc("ve", 1)

        def v_run0(v):
            v.memset(runrow[:], 0.0)
            v.drain()
            v.nop().then_inc(SS["ve"], 1)
        sch.on("vector", v_run0)
        sch.inc("ve", 1)

        for tt in range(NTT):
            def t_lgt(t, tt=tt, vw=sch.get("ve")):
                t.wait_ge(SS["ve"], vw)
                _mm = t.transpose(prt[:, 0:32], lgfm[:, tt * 128:(tt + 1) * 128],
                            identf[0:32, 0:32])
                _mm.then_inc(SS["pe"], 1)
            sch.on("tensor", t_lgt)
            tr_tgt = sch.inc("pe", 1)

            def v_rt(v, tt=tt, tr_tgt=tr_tgt):
                v.wait_ge(SS["pe"], tr_tgt)
                v.tensor_copy(lg_t[:], prt[:, 0:32])
                v.drain()
                v.tensor_reduce(mx[:, 0:1], lg_t[:], op=ALU.max, axis=AX.X)
                v.drain()
                v.tensor_scalar(ohf[:], lg_t[:], mx[:, 0:1], op0=ALU.is_ge,
                                scalar2=None)
                v.drain()
                v.tensor_scalar(tmp32[:], io32[:], 65536.0, op0=ALU.subtract,
                                scalar2=None)
                v.drain()
                v.tensor_tensor(tmp32[:], ohf[:], tmp32[:], op=ALU.mult)
                v.drain()
                v.tensor_reduce(mx[:, 1:2], tmp32[:], op=ALU.min, axis=AX.X)
                v.drain()
                v.tensor_scalar(mx[:, 1:2], mx[:, 1:2], 65536.0, op0=ALU.add,
                                scalar2=None)
                v.drain()
                v.tensor_scalar(ohb[:], io32[:], mx[:, 1:2], op0=ALU.is_equal,
                                scalar2=None)
                v.drain()
                v.nop().then_inc(SS["ve"], 1)
            sch.on("vector", v_rt)
            rt_tgt = sch.inc("ve", 1)

            def s_sig(s, rt_tgt=rt_tgt):
                s.wait_ge(SS["ve"], rt_tgt)
                s.activation(mx[:, 3:4], mx[:, 0:1], AF.Exp, scale=-1.0)
                s.drain()
                s.nop().then_inc(SS["se"], 1)
            sch.on("scalar", s_sig)
            sg_tgt = sch.inc("se", 1)

            def v_sig(v, tt=tt, sg_tgt=sg_tgt):
                v.wait_ge(SS["se"], sg_tgt)
                v.tensor_scalar(mx[:, 4:5], mx[:, 3:4], 1.0, op0=ALU.add,
                                scalar2=None)
                v.drain()
                v.reciprocal(sc_all[:, tt:tt + 1], mx[:, 4:5])
                v.drain()
                v.nop().then_inc(SS["ve"], 1)
            sch.on("vector", v_sig)
            sig_tgt = sch.inc("ve", 1)

            def t_cs(t, sig_tgt=sig_tgt):
                t.wait_ge(SS["ve"], sig_tgt)
                _mm = t.matmul(prt[:, 32:64], ut128[:], ohb[:], start=True, stop=True,
                         skip_group_check=True)
                _mm.then_inc(SS["pe"], 1)
            sch.on("tensor", t_cs)
            cs_tgt = sch.inc("pe", 1)

            def v_cs(v, cs_tgt=cs_tgt):
                v.wait_ge(SS["pe"], cs_tgt)
                v.tensor_copy(wnf[:], prt[:, 32:64])
                v.drain()
                v.nop().then_inc(SS["ve"], 1)
            sch.on("vector", v_cs)
            wn_tgt = sch.inc("ve", 1)

            def t_cn(t, wn_tgt=wn_tgt):
                t.wait_ge(SS["ve"], wn_tgt)
                _mm = t.transpose(prt[0:32, 64:192], wnf[:], identf[:])
                _mm.then_inc(SS["pe"], 1)
            sch.on("tensor", t_cn)
            cn_tgt = sch.inc("pe", 1)

            def v_cn(v, cn_tgt=cn_tgt):
                v.wait_ge(SS["pe"], cn_tgt)
                v.tensor_copy(cntc[:], prt[0:32, 191:192])
                v.drain()
                v.nop().then_inc(SS["ve"], 1)
            sch.on("vector", v_cn)
            cc_tgt = sch.inc("ve", 1)

            def t_cr(t, cc_tgt=cc_tgt):
                t.wait_ge(SS["ve"], cc_tgt)
                _mm = t.transpose(prt[0:1, 192:224], cntc[:], identf[0:32, 0:32])
                _mm = t.matmul(prt[:, 224:256], onesr[:], runrow[:],
                         start=True, stop=True, skip_group_check=True)
                _mm.then_inc(SS["pe"], 1)
            sch.on("tensor", t_cr)
            cr_tgt = sch.inc("pe", 1)

            def v_pv(v, tt=tt, cr_tgt=cr_tgt):
                v.wait_ge(SS["pe"], cr_tgt)
                v.scalar_tensor_tensor(bcr[:], prt[:, 224:256], 1.0, cem1[:],
                                       op0=ALU.mult, op1=ALU.add)
                v.drain()
                v.scalar_tensor_tensor(tmp32[:], prt[:, 32:64], 1.0, bcr[:],
                                       op0=ALU.mult, op1=ALU.add)
                v.drain()
                v.tensor_tensor(bcr[:], tmp32[:], ohb[:], op=ALU.mult)
                v.drain()
                v.tensor_reduce(pv[:, 0:1], bcr[:], op=ALU.add, axis=AX.X)
                v.drain()
                v.tensor_scalar(pv[:, 0:1], pv[:, 0:1], -1.0, op0=ALU.add,
                                scalar2=None)
                v.drain()
                v.tensor_tensor(bcr[:], io32[:], ohb[:], op=ALU.mult)
                v.drain()
                v.tensor_reduce(pv[:, 1:2], bcr[:], op=ALU.add, axis=AX.X)
                v.drain()
                v.tensor_scalar(pv[:, 2:3], pv[:, 1:2], -float(CAP),
                                op0=ALU.mult, scalar2=pv[:, 0:1], op1=ALU.add)
                v.drain()
                v.tensor_scalar(pv[:, 3:4], pv[:, 2:3], float(CAP) - 0.5,
                                op0=ALU.is_ge, scalar2=None)
                v.drain()
                v.tensor_tensor(pv[:, 4:5], tloc[:], pv[:, 0:1],
                                op=ALU.subtract)
                v.drain()
                v.tensor_scalar(pv[:, 5:6], pv[:, 4:5], pv[:, 3:4],
                                op0=ALU.mult, scalar2=pv[:, 0:1], op1=ALU.add)
                v.drain()
                v.tensor_copy(p_all[:, tt:tt + 1], pv[:, 5:6])
                v.drain()
                v.tensor_tensor(runrow[:], runrow[:], prt[0:1, 192:224],
                                op=ALU.add)
                v.drain()
                v.nop().then_inc(SS["ve"], 1)
            sch.on("vector", v_pv)
            pv_tgt = sch.inc("ve", 1)

            def sy_l2(sy, tt=tt, gw=sch.get("d_sct")):
                sy.wait_ge(SS["d_sct"], gw)
                sy.dma_start(ln2t[:], ln2_d[tt * 128:(tt + 1) * 128, :]
                            ).then_inc(SS["d_l2"], 16)
            sch.on("sync", sy_l2)
            l2_tgt = sch.inc("d_l2", 16)

            def v_xs(v, tt=tt, l2_tgt=l2_tgt, pv_tgt=pv_tgt):
                v.wait_ge(SS["d_l2"], l2_tgt)
                v.wait_ge(SS["ve"], pv_tgt)
                v.tensor_scalar(xst[:], ln2t[:], sc_all[:, tt:tt + 1],
                                op0=ALU.mult, scalar2=None)
                v.drain()
                v.nop().then_inc(SS["ve"], 1)
            sch.on("vector", v_xs)
            xs_tgt = sch.inc("ve", 1)

            def g_sc(g, tt=tt, xs_tgt=xs_tgt, zw=gq_init_tgt[0]):
                g.wait_ge(SS["ve"], xs_tgt)
                if tt == 0:
                    g.wait_ge(SS["d_ini"], zw)
                g.indirect_dma_start(
                    out=sorted_d,
                    out_offset=IndirectOffsetOnAxis(ap=p_all[:, tt:tt + 1],
                                                    axis=0),
                    in_=xst[:], in_offset=None).then_inc(SS["d_sct"], 16)
            sch.on("gpsimd", g_sc)
            sch.inc("d_sct", 16)

    def emit_f2(es_ph, ln2fm):
        DEPS = {"g": 16, "u": 16, "d": 16}
        sxt = [sbt(es_ph, f"sxt{i}", [128, 8 * 256], BF16,
                   is_transpose_dest=True) for i in range(2)]
        gbuf = [sbt(es_ph, f"gbuf{i}", [128, IFF], BF16)
                for i in range(DEPS["g"])]
        ubuf = [sbt(es_ph, f"ubuf{i}", [128, IFF], BF16)
                for i in range(DEPS["u"])]
        dbuf = [sbt(es_ph, f"dbuf{i}", [128, D], BF16)
                for i in range(DEPS["d"])]
        gsil = sbt(es_ph, "gsil", [128, IFF], BF16)
        actm = sbt(es_ph, "actm", [128, 2 * IFF], BF16)
        actf = sbt(es_ph, "actf", [128, 16 * 256], BF16)
        dst = sbt(es_ph, "dst", [128, D], BF16)

        s_table(SILUS)
        scatter_gq = sch.get("d_sct")
        fams = {"g": gbuf, "u": ubuf, "d": dbuf}
        pending = {f: [] for f in fams}
        emitted = {f: 0 for f in fams}
        consumed = {f: 0 for f in fams}
        ldinfo = {f: {} for f in fams}
        cons = {f: {} for f in fams}

        def pump(fam):
            DEP = DEPS[fam]
            while (pending[fam] and
                   emitted[fam] < consumed[fam] + DEP):
                n = emitted[fam]
                slot = n % DEP
                w = cons[fam].get(n - DEP, 0)
                src_ap = pending[fam].pop(0)
                bufs = fams[fam]

                def g_ld(g, fam=fam, slot=slot, w=w, src_ap=src_ap,
                         bufs=bufs):
                    g.wait_ge(SS["pe"], w)
                    g.dma_start(bufs[slot][:], src_ap
                                ).then_inc(SS[f"d_{fam}{slot}"], 16)
                sch.on("gpsimd", g_ld)
                ldinfo[fam][n] = (slot, sch.inc(f"d_{fam}{slot}", 16))
                emitted[fam] += 1

        def consume(fam, pe_target):
            cons[fam][consumed[fam]] = pe_target
            consumed[fam] += 1
            for f in ("g", "u", "d"):
                pump(f)

        sx_rdy = [0, 0]

        def emit_sxt(e, slot):
            def s_sx(s, e=e, slot=slot, pw=sch.get("pe"),
                     scatter_gq=scatter_gq):
                s.wait_ge(SS["pe"], pw)
                if e == 0:
                    s.wait_ge(SS["d_sct"], scatter_gq)
                for cc in range(8):
                    s.dma_start(sxt[slot][:, cc * 256:(cc + 1) * 256],
                                sorted_d[e * CAP:(e + 1) * CAP,
                                         cc * 128:(cc + 1) * 128],
                                transpose=True).then_inc(SS[f"d_sx{slot}"], 16)
            sch.on("sync", s_sx)
            return sch.inc(f"d_sx{slot}", 16 * 8)

        sx_rdy[0] = emit_sxt(0, 0)

        for e in range(E):
            slot = e % 2
            if e + 1 < E:
                sx_rdy[(e + 1) % 2] = emit_sxt(e + 1, (e + 1) % 2)
            for c in range(8):
                pending["g"].append(r_g[e, c * 128:(c + 1) * 128, :])
                pending["u"].append(r_u[e, c * 128:(c + 1) * 128, :])
            for c in range(16):
                pending["d"].append(r_d[e, c * 128:(c + 1) * 128, :])
            for f in ("g", "u", "d"):
                pump(f)

            es_gu = ExitStack()
            pgu = es_gu.enter_context(
                nc.psum_tensor(f"pgu{e}", [128, 2 * IFF], F32))
            sil_tgt = {}
            act_tgt_h = {}
            for h in range(2):
                for cin in range(8):
                    lslot, ltgt = ldinfo["g"][e * 8 + cin]

                    def t_g(t, h=h, cin=cin, lslot=lslot, ltgt=ltgt, pgu=pgu,
                            sx_t=sx_rdy[slot], slot=slot,
                            sw=sil_tgt.get(("s", h - 1), 0),
                            vw=sch.get("ve")):
                        t.wait_ge(SS[f"d_g{lslot}"], ltgt)
                        if cin == 0:
                            if h == 0:
                                t.wait_ge(SS[f"d_sx{slot}"], sx_t)
                                t.wait_ge(SS["ve"], vw)
                            else:
                                t.wait_ge(SS["se"], sw)
                        for pt_i in range(2):
                            for nb in range(2):
                                _mm = t.matmul(
                                    pgu[:, (pt_i * 2 + nb) * 512:
                                        (pt_i * 2 + nb + 1) * 512],
                                    sxt[slot][:, cin * 256 + pt_i * 128:
                                              cin * 256 + (pt_i + 1) * 128],
                                    gbuf[lslot][:, h * 1024 + nb * 512:
                                                h * 1024 + (nb + 1) * 512],
                                    start=(cin == 0), stop=(cin == 7))
                        _mm.then_inc(SS["pe"], 1)
                    sch.on("tensor", t_g)
                    tgt = sch.inc("pe", 1)
                    if h == 1:
                        consume("g", tgt)
                g_done = sch.get("pe")

                for cin in range(8):
                    lslot, ltgt = ldinfo["u"][e * 8 + cin]

                    def t_u(t, h=h, cin=cin, lslot=lslot, ltgt=ltgt, slot=slot,
                            pgu=pgu, aw=act_tgt_h.get(h - 1, 0)):
                        t.wait_ge(SS[f"d_u{lslot}"], ltgt)
                        if cin == 0 and h == 1:
                            t.wait_ge(SS["ve"], aw)
                        for pt_i in range(2):
                            for nb in range(2):
                                _mm = t.matmul(
                                    pgu[:, IFF + (pt_i * 2 + nb) * 512:
                                        IFF + (pt_i * 2 + nb + 1) * 512],
                                    sxt[slot][:, cin * 256 + pt_i * 128:
                                              cin * 256 + (pt_i + 1) * 128],
                                    ubuf[lslot][:, h * 1024 + nb * 512:
                                                h * 1024 + (nb + 1) * 512],
                                    start=(cin == 0), stop=(cin == 7))
                        _mm.then_inc(SS["pe"], 1)
                    sch.on("tensor", t_u)
                    tgt = sch.inc("pe", 1)
                    if h == 1:
                        consume("u", tgt)
                u_done = sch.get("pe")

                def s_sil(s, h=h, g_done=g_done, pgu=pgu,
                          vw=sch.get("ve")):
                    s.wait_ge(SS["pe"], g_done)
                    s.wait_ge(SS["ve"], vw)
                    s.activation(gsil[:, 0:IFF], pgu[:, 0:IFF], AF.Silu)
                    s.drain()
                    s.nop().then_inc(SS["se"], 1)
                sch.on("scalar", s_sil)
                sil_tgt[("s", h)] = sch.inc("se", 1)

                def v_act(v, h=h, pgu=pgu, st=sil_tgt[("s", h)],
                          uw=u_done):
                    v.wait_ge(SS["se"], st)
                    v.wait_ge(SS["pe"], uw)
                    v.tensor_tensor(
                        actm.rearrange("p (a i) -> p a i", a=2)
                        [:, :, h * 1024:(h + 1) * 1024],
                        gsil.rearrange("p (a i) -> p a i", a=2)
                        [:, :, 0:1024] if False else
                        gsil[:, 0:IFF].rearrange("p (a i) -> p a i", a=2),
                        pgu[:, IFF:2 * IFF].rearrange("p (a i) -> p a i", a=2),
                        op=ALU.mult)
                    v.drain()
                    v.nop().then_inc(SS["ve"], 1)
                sch.on("vector", v_act)
                act_tgt_h[h] = sch.inc("ve", 1)
            act_tgt = act_tgt_h[1]
            if F2DBG and e == 0:
                def sy_dbg1(sy, act_tgt=act_tgt):
                    sy.wait_ge(SS["ve"], act_tgt)
                    sy.dma_start(f2dbg_d[:, 0:2 * IFF], actm[:]
                                 ).then_inc(SS["d_dbg"], 16)
                    sy.dma_start(f2dbg_d[:, 2 * IFF:3 * IFF], gsil[:]
                                 ).then_inc(SS["d_dbg"], 16)
                sch.on("sync", sy_dbg1)
                sch.inc("d_dbg", 32)
            es_gu.close()

            es_dn = ExitStack()
            ptx = es_dn.enter_context(
                nc.psum_tensor(f"ptx{e}", [128, 4096], BF16))
            pdn = [es_dn.enter_context(
                nc.psum_tensor(f"pdn{e}_{p}", [128, D], F32)) for p in range(2)]
            vbase = sch.get("ve")
            vtx_tgt = {}
            for ii in range(16):
                for pt_i in range(2):
                    iin = ii * 2 + pt_i

                    def t_tx(t, pt_i=pt_i, ii=ii, iin=iin, act_tgt=act_tgt,
                             ptx=ptx, vv=max(vbase, sch.get("ve") - 3)):
                        t.wait_ge(SS["ve"], act_tgt)
                        t.wait_ge(SS["ve"], vv)
                        _mm = t.transpose(ptx[:, (iin % 4) * 1024:
                                              (iin % 4) * 1024 + 128],
                                          actm[:, pt_i * IFF + ii * 128:
                                               pt_i * IFF + (ii + 1) * 128],
                                          ident[:])
                        _mm.then_inc(SS["pe"], 1)
                    sch.on("tensor", t_tx)
                    tx_tgt = sch.inc("pe", 1)

                    def v_tx(v, pt_i=pt_i, ii=ii, iin=iin, tx_tgt=tx_tgt,
                             ptx=ptx):
                        v.wait_ge(SS["pe"], tx_tgt)
                        v.tensor_copy(actf[:, ii * 256 + pt_i * 128:
                                          ii * 256 + (pt_i + 1) * 128],
                                      ptx[:, (iin % 4) * 1024:
                                          (iin % 4) * 1024 + 128])
                        v.drain()
                        v.nop().then_inc(SS["ve"], 1)
                    sch.on("vector", v_tx)
                    vtx_tgt[ii] = sch.inc("ve", 1)
            actf_ve = sch.get("ve")
            if F2DBG and e == 0:
                def sy_dbg2(sy, actf_ve=actf_ve):
                    sy.wait_ge(SS["ve"], actf_ve)
                    sy.dma_start(f2dbg_d[:, 3 * IFF:5 * IFF], actf[:]
                                 ).then_inc(SS["d_dbg"], 16)
                sch.on("sync", sy_dbg2)
                sch.inc("d_dbg", 16)

            for cin in range(16):
                lslot, ltgt = ldinfo["d"][e * 16 + cin]

                def t_d2(t, cin=cin, lslot=lslot, ltgt=ltgt, pdn=pdn,
                         aw=vtx_tgt[cin], vw=sch.get("ve")):
                    t.wait_ge(SS[f"d_d{lslot}"], ltgt)
                    t.wait_ge(SS["ve"], aw)
                    if cin == 0:
                        t.wait_ge(SS["ve"], vw)
                    for pt_i in range(2):
                        for half in range(2):
                            _mm = t.matmul(
                                pdn[pt_i][:, half * 512:(half + 1) * 512],
                                actf[:, cin * 256 + pt_i * 128:
                                     cin * 256 + (pt_i + 1) * 128],
                                dbuf[lslot][:, half * 512:(half + 1) * 512],
                                start=(cin == 0), stop=(cin == 15))
                    _mm.then_inc(SS["pe"], 1)
                sch.on("tensor", t_d2)
                consume("d", sch.inc("pe", 1))
            dn_done = sch.get("pe")

            for pt_i in range(2):
                def v_dn(v, pt_i=pt_i, dn_done=dn_done, pdn=pdn,
                         sw=sch.get("d_dst")):
                    v.wait_ge(SS["pe"], dn_done)
                    v.wait_ge(SS["d_dst"], sw)
                    v.tensor_copy(dst[:], pdn[pt_i][:])
                    v.drain()
                    v.nop().then_inc(SS["ve"], 1)
                sch.on("vector", v_dn)
                dn_ve = sch.inc("ve", 1)

                def sy_dn(sy, e=e, pt_i=pt_i, dn_ve=dn_ve):
                    sy.wait_ge(SS["ve"], dn_ve)
                    sy.dma_start(dexp_d[e * CAP + pt_i * 128:
                                        e * CAP + (pt_i + 1) * 128, :], dst[:]
                                 ).then_inc(SS["d_dst"], 16)
                sch.on("sync", sy_dn)
                sch.inc("d_dst", 16)
            es_dn.close()

    def emit_f14(es_ph, ln2fm):
        ln2fm = [sbt(es_ph, f"f1ln2fm{c}", [128, N], BF16,
                     is_transpose_dest=True) for c in range(8)]

        def s_trl2(s):
            for c in range(8):
                s.dma_start(ln2fm[c][:], ln2_d[:, c * 128:(c + 1) * 128],
                            transpose=True).then_inc(SS["d_trl"], 16)
        sch.on("scalar", s_trl2)
        trl2_tgt = sch.inc("d_trl", 16 * 8)
        shg = [sbt(es_ph, f"shg{c}", [128, IFF], BF16) for c in range(8)]
        shu = [sbt(es_ph, f"shu{c}", [128, IFF], BF16) for c in range(8)]
        shd = [sbt(es_ph, f"shd{c}", [128, D], BF16) for c in range(16)]
        gsil = sbt(es_ph, "f1gsil", [128, IFF], BF16)
        actm = sbt(es_ph, "f1actm", [128, IFF], BF16)
        actf = sbt(es_ph, "f1actf", [128, 16 * 128], BF16)
        rtt = sbt(es_ph, "rtt", [128, D], BF16)
        xmt = sbt(es_ph, "xmt", [128, D], BF16)
        fo = sbt(es_ph, "fo", [128, D], F32)

        def g_ldw(g):
            for c in range(8):
                g.dma_start(shg[c][:], sh_g[c * 128:(c + 1) * 128, :]
                            ).then_inc(SS["d_shw"], 16)
                g.dma_start(shu[c][:], sh_u[c * 128:(c + 1) * 128, :]
                            ).then_inc(SS["d_shw"], 16)
            for c in range(16):
                g.dma_start(shd[c][:], sh_d[c * 128:(c + 1) * 128, :]
                        ).then_inc(SS["d_shw"], 16)
        sch.on("gpsimd", g_ldw)
        w_tgt = sch.inc("d_shw", 16 * 32)

        pg = es_ph.enter_context(nc.psum_tensor("f1pg", [128, 1024], F32))
        pu_ = es_ph.enter_context(nc.psum_tensor("f1pu", [128, 1024], F32))
        ptx = es_ph.enter_context(nc.psum_tensor("f1ptx", [128, 2048], BF16))
        pdn = es_ph.enter_context(nc.psum_tensor("f1pdn", [128, 1024], F32))

        sil_t = {}
        act_t = {}
        gu_g_t = {}
        gu_u_t = {}
        txp_t = {}
        vtx_t = {}
        dnc_t = {}
        fin_t = {}
        out_t = {}

        for tt in range(NTT):
            for h in range(2):
                k = tt * 2 + h
                kp = ((k - 1) // 2, (k - 1) % 2)

                def t_g(t, tt=tt, h=h, w_tgt=w_tgt, lt=trl2_tgt,
                        sw=sil_t.get(((tt * 2 + h - 1) // 2,
                                      (tt * 2 + h - 1) % 2), 0)):
                    if tt == 0 and h == 0:
                        t.wait_ge(SS["d_shw"], w_tgt)
                        t.wait_ge(SS["d_trl"], lt)
                    t.wait_ge(SS["se"], sw)
                    for cin in range(8):
                        for nb in range(2):
                            _mm = t.matmul(
                                pg[:, nb * 512:(nb + 1) * 512],
                                ln2fm[cin][:, tt * 128:(tt + 1) * 128],
                                shg[cin][:, h * 1024 + nb * 512:
                                          h * 1024 + (nb + 1) * 512],
                                start=(cin == 0), stop=(cin == 7))
                    _mm.then_inc(SS["pe"], 1)
                sch.on("tensor", t_g)
                gu_g_t[(tt, h)] = sch.inc("pe", 1)

                def t_u(t, tt=tt, h=h, aw=act_t.get(kp, 0)):
                    t.wait_ge(SS["ve"], aw)
                    for cin in range(8):
                        for nb in range(2):
                            _mm = t.matmul(
                                pu_[:, nb * 512:(nb + 1) * 512],
                                ln2fm[cin][:, tt * 128:(tt + 1) * 128],
                                shu[cin][:, h * 1024 + nb * 512:
                                          h * 1024 + (nb + 1) * 512],
                                start=(cin == 0), stop=(cin == 7))
                    _mm.then_inc(SS["pe"], 1)
                sch.on("tensor", t_u)
                gu_u_t[(tt, h)] = sch.inc("pe", 1)

                def s_sil(s, tt=tt, h=h, gw=gu_g_t[(tt, h)],
                          aw=act_t.get((tt - 1, h), 0)):
                    s.wait_ge(SS["pe"], gw)
                    s.wait_ge(SS["ve"], aw)
                    s.activation(gsil[:, h * 1024:(h + 1) * 1024], pg[:],
                                 AF.Silu)
                    s.drain()
                    s.nop().then_inc(SS["se"], 1)
                sch.on("scalar", s_sil)
                sil_t[(tt, h)] = sch.inc("se", 1)

                def v_act(v, tt=tt, h=h, st=sil_t[(tt, h)],
                          uw=gu_u_t[(tt, h)], tw=txp_t.get((tt - 1, h), 0)):
                    v.wait_ge(SS["se"], st)
                    v.wait_ge(SS["pe"], uw)
                    v.wait_ge(SS["pe"], tw)
                    v.tensor_tensor(actm[:, h * 1024:(h + 1) * 1024],
                                    gsil[:, h * 1024:(h + 1) * 1024],
                                    pu_[:], op=ALU.mult)
                    v.drain()
                    v.nop().then_inc(SS["ve"], 1)
                sch.on("vector", v_act)
                act_t[(tt, h)] = sch.inc("ve", 1)

            for h in range(2):
                for k in range(8):
                    ii = h * 8 + k
                    q = (tt * 2 + h) * 4 + k // 2

                    def t_tx(t, tt=tt, h=h, k=k, ii=ii,
                             aw=act_t[(tt, h)],
                             vv=vtx_t.get(q - 1, 0)):
                        t.wait_ge(SS["ve"], aw)
                        t.wait_ge(SS["ve"], vv)
                        _mm = t.transpose(ptx[:, (k % 2) * 1024:
                                              (k % 2) * 1024 + 128],
                                          actm[:, ii * 128:(ii + 1) * 128],
                                          ident[:])
                        _mm.then_inc(SS["pe"], 1)
                    sch.on("tensor", t_tx)
                    tx_tgt = sch.inc("pe", 1)
                    if k % 2 == 1:
                        def v_tx(v, tt=tt, h=h, k=k, ii=ii, tx_tgt=tx_tgt,
                                 dw=dnc_t.get((tt - 1, ii), 0)):
                            v.wait_ge(SS["pe"], tx_tgt)
                            v.wait_ge(SS["pe"], dw)
                            v.tensor_copy(
                                actf[:, (ii - 1) * 128:(ii + 1) * 128]
                                .rearrange("p (a c) -> p a c", a=2),
                                ptx.rearrange("p (a c) -> p a c", a=2)
                                [:, :, 0:128])
                            v.drain()
                            v.nop().then_inc(SS["ve"], 1)
                        sch.on("vector", v_tx)
                        vtx_t[q] = sch.inc("ve", 1)
                txp_t[(tt, h)] = sch.get("pe")

            def g_ga(g, tt=tt, fw=fin_t.get(tt - 1, 0),
                     dw=sch.get("d_dst")):
                g.wait_ge(SS["ve"], fw)
                if tt == 0:
                    g.wait_ge(SS["d_dst"], dw)
                g.indirect_dma_start(
                    out=rtt[:], out_offset=None,
                    in_=dexp_d,
                    in_offset=IndirectOffsetOnAxis(ap=p_all[:, tt:tt + 1],
                                                   axis=0)).then_inc(SS["d_ga"], 16)
            sch.on("gpsimd", g_ga)
            ga_tgt = sch.inc("d_ga", 16)

            def sy_xm(sy, tt=tt, fw=fin_t.get(tt - 1, 0)):
                sy.wait_ge(SS["ve"], fw)
                sy.dma_start(xmt[:], xmid_d[tt * 128:(tt + 1) * 128, :]
                            ).then_inc(SS["d_xt"], 16)
            sch.on("sync", sy_xm)
            xm_tgt = sch.inc("d_xt", 16)

            for cin in range(16):
                def t_dn(t, tt=tt, cin=cin,
                         vw=vtx_t[(tt * 2 + cin // 8) * 4 + (cin % 8) // 2],
                         fw=fin_t.get(tt - 1, 0)):
                    t.wait_ge(SS["ve"], vw)
                    if cin == 0:
                        t.wait_ge(SS["ve"], fw)
                    _mm = t.matmul(pdn[:, 0:512],
                                   actf[:, cin * 128:(cin + 1) * 128],
                                   shd[cin][:, 0:512],
                                   start=(cin == 0), stop=(cin == 15))
                    _mm = t.matmul(pdn[:, 512:1024],
                                   actf[:, cin * 128:(cin + 1) * 128],
                                   shd[cin][:, 512:1024],
                                   start=(cin == 0), stop=(cin == 15))
                    _mm.then_inc(SS["pe"], 1)
                sch.on("tensor", t_dn)
                dnc_t[(tt, cin)] = sch.inc("pe", 1)
            dn_done = sch.get("pe")

            def v_fin(v, tt=tt, dn_done=dn_done, ga_tgt=ga_tgt,
                      xm_tgt=xm_tgt, ow=out_t.get(tt - 1, 0)):
                v.wait_ge(SS["pe"], dn_done)
                v.wait_ge(SS["d_ga"], ga_tgt)
                v.wait_ge(SS["d_xt"], xm_tgt)
                v.wait_ge(SS["d_out"], ow)
                v.scalar_tensor_tensor(fo[:], pdn[:], 1.0, rtt[:],
                                       op0=ALU.mult, op1=ALU.add)
                v.drain()
                v.tensor_tensor(fo[:], fo[:], xmt[:], op=ALU.add)
                v.drain()
                v.nop().then_inc(SS["ve"], 1)
            sch.on("vector", v_fin)
            fin_t[tt] = sch.inc("ve", 1)

            def sy_out(sy, tt=tt, fin_tgt=fin_t[tt]):
                sy.wait_ge(SS["ve"], fin_tgt)
                sy.dma_start(out_d[tt * 128:(tt + 1) * 128, :], fo[:]
                            ).then_inc(SS["d_out"], 16)
            sch.on("sync", sy_out)
            out_t[tt] = sch.inc("d_out", 16)

    order = ["init", "p0", "p1", "p2", "f0a", "f0", "f2", "f14"]
    lim = order.index(upto) if upto in order else len(order)

    def want(ph):
        return order.index(ph) <= lim if upto != "all" else True
    with ExitStack() as es_i:
        emit_init(es_i)
        sch.barrier()
    if want("p0"):
        with ExitStack() as es0:
            emit_ln_pass(es0, "p0", x_in, ln1_d, F32)
            sch.barrier()
    if want("p1"):
        with ExitStack() as es_at:
            vtm = sbt(es_at, "vtm", [128, NJT * 1040], BF16)
            with ExitStack() as es1:
                emit_p1(es1, vtm)
                sch.barrier()
            if want("p2"):
                with ExitStack() as es2:
                    emit_p2(es2, vtm)
                    sch.barrier()
    if want("f0a"):
        with ExitStack() as esf:
            with ExitStack() as esl:
                emit_ln_pass(esl, "f0", xmid_d, ln2_d, BF16)
                sch.barrier()
            if want("f0"):
                with ExitStack() as esr:
                    ln2fm = [sbt(esr, f"ln2fm{c}", [128, N], BF16,
                                 is_transpose_dest=True) for c in range(8)]
                    emit_f0(esr, ln2fm)
                    sch.barrier()
            if want("f2"):
                with ExitStack() as ese:
                    emit_f2(ese, None)
                    sch.barrier()
            if want("f14"):
                with ExitStack() as es4:
                    emit_f14(es4, None)

    dma_targets = {nm: sch.get(nm) for nm in dma_sem_names if sch.get(nm)}

    def sy_fin(sy, dma_targets=dma_targets):
        for nm, v in dma_targets.items():
            sy.wait_ge(SS[nm], v)
    sch.on("sync", sy_fin)

    with nc.Block() as blk:
        sch.emit(blk)
    es_top.close()
    return nc


_NC_CACHE = {}


def kernel(**inputs):
    if "main" not in _NC_CACHE:
        _NC_CACHE["main"] = build_nc()
    nc = _NC_CACHE["main"]
    consts = host_consts()
    x = np.ascontiguousarray(inputs["L0_attn_x"])
    in_maps = []
    for core in range(NCORE):
        m = {
            "x": np.ascontiguousarray(x[core]),
            "Wq": inputs["L0_attn_Wq"], "Wk": inputs["L0_attn_Wk"],
            "Wv": inputs["L0_attn_Wv"], "Wproj": inputs["L0_attn_Wproj"],
            "bproj": inputs["L0_attn_bproj"], "Wpre": inputs["L0_attn_Wpre"],
            "bpre": inputs["L0_attn_bpre"], "pi": inputs["L0_attn_pi"],
            "gate_w": inputs["L0_ffn_gate_w"],
            "sh_g": inputs["L0_ffn_shared_gate_w"],
            "sh_u": inputs["L0_ffn_shared_up_w"],
            "sh_d": inputs["L0_ffn_shared_down_w"],
            "r_g": inputs["L0_ffn_routed_gate_w"],
            "r_u": inputs["L0_ffn_routed_up_w"],
            "r_d": inputs["L0_ffn_routed_down_w"],
        }
        for k, v in consts.items():
            m["c_" + k] = v
        in_maps.append(m)
    res = run_bass_kernel_spmd(nc, in_maps, core_ids=list(range(NCORE)))
    out = np.stack([r["out"] for r in res.results], axis=0)
    return out.astype(np.float32)


# version 21 (final)
# speedup vs baseline: 1.0313x; 1.0313x over previous
import numpy as np
from contextlib import ExitStack

import concourse.bass as bass
import concourse.mybir as mybir
from concourse.bass import IndirectOffsetOnAxis
from concourse.bass_utils import run_bass_kernel_spmd

F32 = mybir.dt.float32
BF16 = mybir.dt.bfloat16
U32 = mybir.dt.uint32
AF = mybir.ActivationFunctionType
ALU = mybir.AluOpType
AX = mybir.AxisListType

N = 4096
D = 1024
P1_NW = 4
F2DBG = False
P1_PART = "post"
H = 16
DH = 64
IC = 256
NCH = N // IC
NJT = N // 128
NTT = N // 128
E = 32
CAP = 256
PPAD = E * CAP
PTOT = PPAD + 128
IFF = 2048
LN_EPS = 1e-5
NCORE = 8
ENGINES = ("tensor", "vector", "scalar", "gpsimd", "sync")


def _table_id(nc, name):
    from concourse.hw_specs import get_activation_tables
    return list(get_activation_tables(nc.m.arch).keys()).index(name)


def host_consts():
    c = {}
    c["ident"] = np.eye(128, dtype=np.float32)
    c["onesr"] = np.ones((1, 128), dtype=np.float32)
    c["ut128"] = np.triu(np.ones((128, 128), dtype=np.float32))
    hma = np.zeros((128, 128), dtype=np.float32)
    for cc in range(8):
        for r in range(128):
            hma[r, cc * 16 + 2 * cc + r // 64] = 1.0
    c["hmall"] = hma
    c["io32"] = np.tile(np.arange(32, dtype=np.float32), (128, 1))
    c["cem1"] = np.tile(CAP * np.arange(32, dtype=np.float32), (128, 1))
    c["onesc"] = np.ones((128, 1), dtype=np.float32)
    c["tloc"] = (PPAD + np.arange(128, dtype=np.float32))[:, None]
    return c


class Sched:

    def __init__(self, nc):
        self.nc = nc
        self.steps = {e: [] for e in ENGINES}
        self.n = {}
        self.sems = {}

    def on(self, eng, fn):
        self.steps[eng].append(fn)

    def inc(self, k, amt=1):
        self.n[k] = self.n.get(k, 0) + amt
        return self.n[k]

    def get(self, k):
        return self.n.get(k, 0)

    def emit(self, blk):
        for eng in ENGINES:
            steps = self.steps[eng]

            def mk(steps):
                def _run(e):
                    for fn in steps:
                        fn(e)
                return _run
            getattr(blk, eng)(mk(steps))

    def barrier(self):
        snap = {}
        for eng in ENGINES:
            k = "b_" + eng
            tgt = self.inc(k, 1)
            sem = self.sems[k]

            def f(e, sem=sem):
                e.drain()
                e.nop().then_inc(sem, 1)
            self.on(eng, f)
            snap[k] = tgt
        for eng in ENGINES:
            def w(e, snap=dict(snap), sems=self.sems):
                for k, v in snap.items():
                    e.wait_ge(sems[k], v)
            self.on(eng, w)


def build_nc(taps=(), upto="all"):
    nc = bass.Bass("TRN2", target_bir_lowering=False)
    sch = Sched(nc)

    def dram(name, shape, dtype, kind="Internal"):
        k = "ExternalOutput" if name in taps else kind
        return nc.dram_tensor(name, shape, dtype, kind=k).ap()

    x_in = nc.dram_tensor("x", [N, D], F32, kind="ExternalInput").ap()
    Wq = nc.dram_tensor("Wq", [D, D], F32, kind="ExternalInput").ap()
    Wk = nc.dram_tensor("Wk", [D, D], F32, kind="ExternalInput").ap()
    Wv = nc.dram_tensor("Wv", [D, D], F32, kind="ExternalInput").ap()
    Wproj = nc.dram_tensor("Wproj", [D, D], F32, kind="ExternalInput").ap()
    bproj = nc.dram_tensor("bproj", [D], F32, kind="ExternalInput").ap()
    Wpre = nc.dram_tensor("Wpre", [D, D], F32, kind="ExternalInput").ap()
    bpre = nc.dram_tensor("bpre", [D], F32, kind="ExternalInput").ap()
    pi_in = nc.dram_tensor("pi", [1], F32, kind="ExternalInput").ap()
    gate_w = nc.dram_tensor("gate_w", [D, E], F32, kind="ExternalInput").ap()
    sh_g = nc.dram_tensor("sh_g", [D, IFF], F32, kind="ExternalInput").ap()
    sh_u = nc.dram_tensor("sh_u", [D, IFF], F32, kind="ExternalInput").ap()
    sh_d = nc.dram_tensor("sh_d", [IFF, D], F32, kind="ExternalInput").ap()
    r_g = nc.dram_tensor("r_g", [E, D, IFF], F32, kind="ExternalInput").ap()
    r_u = nc.dram_tensor("r_u", [E, D, IFF], F32, kind="ExternalInput").ap()
    r_d = nc.dram_tensor("r_d", [E, IFF, D], F32, kind="ExternalInput").ap()
    ci = {}
    for k, v in host_consts().items():
        ci[k] = nc.dram_tensor("c_" + k, list(v.shape), F32,
                               kind="ExternalInput").ap()
    out_d = nc.dram_tensor("out", [N, D], F32, kind="ExternalOutput").ap()

    ln1_d = dram("ln1_d", [N, D], BF16)
    kt_d = dram("kt_d", [D, N], BF16)
    qa_d = dram("qa_d", [D, N], BF16)
    pre_d = dram("pre_d", [D, N], BF16)
    xmid_d = dram("xmid_d", [N, D], BF16)
    ln2_d = dram("ln2_d", [N, D], BF16)
    sorted_d = dram("sorted_d", [PTOT, D], BF16)
    dexp_d = dram("dexp_d", [PTOT, D], BF16)
    f2dbg_d = dram("f2dbg", [128, 12288], BF16) if F2DBG else None

    es_top = ExitStack()

    def sbt(esx, name, shape, dtype, **kw):
        return esx.enter_context(nc.sbuf_tensor(name, shape, dtype, **kw))

    ident = sbt(es_top, "ident", [128, 128], BF16)
    sid = sbt(es_top, "sid", [128, 128], BF16)
    onesr = sbt(es_top, "onesr", [1, 128], F32)
    ut128 = sbt(es_top, "ut128", [128, 128], BF16)
    hmall = sbt(es_top, "hmall", [128, 128], BF16)
    io32 = sbt(es_top, "io32", [128, 32], BF16)
    cem1 = sbt(es_top, "cem1", [128, 32], BF16)
    onesc = sbt(es_top, "onesc", [128, 1], BF16)
    onesrb = sbt(es_top, "onesrb", [1, 128], BF16)
    identf = sbt(es_top, "identf", [128, 128], F32)
    smT = sbt(es_top, "smT", [128, 2], F32)
    tloc = smT[:, 0:1]
    sprev = smT[:, 1:2]
    bprec = sbt(es_top, "bprec", [128, 8], F32)
    smA = sbt(es_top, "smA", [128, 6], F32)
    uall = sbt(es_top, "uall", [16, N], BF16)
    p_all = sbt(es_top, "p_all", [128, NTT], U32)
    sc_all = sbt(es_top, "sc_all", [128, NTT], F32)

    dma_sem_names = [
        "d_ini", "d_ini2", "d_xb0", "d_xb1", "d_st0", "d_st1",
        "d_w0", "d_w1", "d_sg0", "d_sg1", "d_trl", "d_p1st",
        "d_kt", "d_q", "d_pj0", "d_pj1", "d_wp", "d_x", "d_xm",
        "d_l2", "d_sct", "d_sx0", "d_sx1", "d_g0", "d_g1", "d_g2", "d_g3", "d_g4", "d_g5", "d_g6", "d_g7", "d_g8", "d_g9", "d_g10", "d_g11", "d_g12", "d_g13", "d_g14", "d_g15",
        "d_u0", "d_u1", "d_u2", "d_u3", "d_u4", "d_u5", "d_u6", "d_u7", "d_u8", "d_u9", "d_u10", "d_u11", "d_u12", "d_u13", "d_u14", "d_u15", "d_d0", "d_d1", "d_d2", "d_d3", "d_d4", "d_d5", "d_d6", "d_d7", "d_d8", "d_d9", "d_d10", "d_d11", "d_d12", "d_d13", "d_d14", "d_d15",
        "d_dst", "d_shw", "d_dbg",
        "d_ga0", "d_ga1", "d_xt0", "d_xt1", "d_out0", "d_out1"]
    for nm in (["pe", "ve", "se"] + ["b_" + e for e in ENGINES]
               + dma_sem_names):
        sch.sems[nm] = es_top.enter_context(nc.semaphore(nm))
    SS = sch.sems
    LNEXP = "natural_log_exp_and_others"
    SILUS = "silu_and_others"

    def s_table(name):
        def f(s, name=name):
            s.add_instruction(mybir.InstLoadActFuncSet(
                name=nc.get_next_instruction_name(),
                act_func_set_id=_table_id(nc, name), ins=[], outs=[]))
        sch.on("scalar", f)

    gq_init_tgt = [0]

    def emit_init(es_ph):
        ztile = sbt(es_ph, "ztile", [128, D], BF16)

        def g_init(g):
            g.dma_start(ident[:], ci["ident"]).then_inc(SS["d_ini"], 16)
            g.dma_start(ut128[:], ci["ut128"]).then_inc(SS["d_ini"], 16)
            g.dma_start(hmall[:], ci["hmall"]).then_inc(SS["d_ini"], 16)
            g.dma_start(onesc[:], ci["onesc"]).then_inc(SS["d_ini"], 16)
            g.dma_start(onesrb[:], ci["onesr"]).then_inc(SS["d_ini"], 16)
            g.dma_start(cem1[:], ci["cem1"]).then_inc(SS["d_ini"], 16)
            g.dma_start(io32[:], ci["io32"]).then_inc(SS["d_ini"], 16)
            g.memset(ztile[:], 0.0)
            g.drain()
            g.dma_start(dexp_d[PPAD:PTOT, :], ztile[:]).then_inc(SS["d_ini"], 16)
            for zz in range(PTOT // 128):
                g.dma_start(sorted_d[zz * 128:(zz + 1) * 128, :], ztile[:]
                            ).then_inc(SS["d_ini"], 16)
        sch.on("gpsimd", g_init)
        gq0 = sch.inc("d_ini", 16 * (8 + PTOT // 128))
        gq_init_tgt[0] = gq0

        def sy_init(sy):
            sy.dma_start(onesr[:], ci["onesr"]).then_inc(SS["d_ini2"], 16)
            sy.dma_start(tloc[:], ci["tloc"]).then_inc(SS["d_ini2"], 16)
            sy.dma_start(identf[:], ci["ident"]).then_inc(SS["d_ini2"], 16)
            for cc in range(8):
                sy.dma_start(bprec[:, cc:cc + 1],
                             bpre[cc * 128:(cc + 1) * 128][:, None]
                             ).then_inc(SS["d_ini2"], 16)
            sy.dma_start(smA[0:1, 0:1], pi_in[0:1][None, :]).then_inc(SS["d_ini2"], 16)
        sch.on("sync", sy_init)
        sq0 = sch.inc("d_ini2", 16 * 12)

        def v_pi(v, sq0=sq0):
            v.wait_ge(SS["d_ini2"], sq0)
            v.tensor_tensor(smA[0:1, 1:2], smA[0:1, 0:1], smA[0:1, 0:1],
                            op=ALU.mult)
            v.drain()
            v.nop().then_inc(SS["ve"], 1)
        sch.on("vector", v_pi)
        ve0 = sch.inc("ve", 1)

        s_table(LNEXP)

        def s_pi1(s, ve0=ve0):
            s.wait_ge(SS["ve"], ve0)
            s.activation(smA[0:1, 2:3], smA[0:1, 1:2], AF.Ln)
            s.drain()
            s.nop().then_inc(SS["se"], 1)
        sch.on("scalar", s_pi1)
        se_a = sch.inc("se", 1)

        def v_pi2(v, se_a=se_a):
            v.wait_ge(SS["se"], se_a)
            v.tensor_scalar(smA[0:1, 2:3], smA[0:1, 2:3], 0.25, op0=ALU.mult,
                            scalar2=float(-0.5 * np.log(32.0)), op1=ALU.add)
            v.drain()
            v.nop().then_inc(SS["ve"], 1)
        sch.on("vector", v_pi2)
        ve_b = sch.inc("ve", 1)

        def s_pi(s, ve_b=ve_b):
            s.wait_ge(SS["ve"], ve_b)
            s.activation(smA[0:1, 3:4], smA[0:1, 2:3], AF.Exp)
            s.sign(smA[0:1, 4:5], smA[0:1, 0:1])
            s.drain()
            s.nop().then_inc(SS["se"], 1)
        sch.on("scalar", s_pi)
        se0 = sch.inc("se", 1)

        ps = es_ph.enter_context(nc.psum_tensor("p_ini", [128, 512], F32))

        def t_bc(t, se0=se0, gq0=gq0):
            t.wait_ge(SS["se"], se0)
            t.wait_ge(SS["d_ini"], gq0)
            _mm = t.matmul(ps[:, 0:1], onesr[:], smA[0:1, 3:4], start=True, stop=True)
            _mm = t.matmul(ps[:, 1:2], onesr[:], smA[0:1, 4:5], start=True, stop=True)
            _mm.then_inc(SS["pe"], 1)
        sch.on("tensor", t_bc)
        pe0 = sch.inc("pe", 1)

        def v_bc(v, pe0=pe0):
            v.wait_ge(SS["pe"], pe0)
            v.tensor_copy(sprev[:], ps[:, 0:1])
            v.tensor_scalar(sid[:], ident[:], ps[:, 1:2], op0=ALU.mult,
                            scalar2=None)
            v.drain()
            v.nop().then_inc(SS["ve"], 1)
        sch.on("vector", v_bc)
        sch.inc("ve", 1)


    def emit_ln_pass(es_ph, tag, src_dram, dst_dram, src_dtype):
        xb = [sbt(es_ph, f"{tag}x{i}", [128, D], src_dtype) for i in range(2)]
        lnb = [sbt(es_ph, f"{tag}l{i}", [128, D], BF16) for i in range(2)]
        st = [sbt(es_ph, f"{tag}s{i}", [128, 6], F32) for i in range(2)]
        sq_t = sbt(es_ph, f"{tag}q", [128, D], F32)
        for i in range(NTT):
            b = i % 2
            lsem = f"d_xb{b}"
            ssem = f"d_st{b}"

            def sy_ld(sy, i=i, b=b, lsem=lsem, ssem=ssem,
                      w=sch.get(ssem), sw=sch.get("se")):
                sy.wait_ge(SS[ssem], w)
                sy.wait_ge(SS["se"], sw)
                sy.dma_start(xb[b][:], src_dram[i * 128:(i + 1) * 128, :]
                             ).then_inc(SS[lsem], 16)
            sch.on("sync", sy_ld)
            ld_tgt = sch.inc(lsem, 16)

            def v_mean(v, b=b, lsem=lsem, ld_tgt=ld_tgt, sw=sch.get("se")):
                v.wait_ge(SS[lsem], ld_tgt)
                v.wait_ge(SS["se"], sw)
                v.tensor_reduce(st[b][:, 0:1], xb[b][:], op=ALU.add, axis=AX.X)
                v.drain()
                v.tensor_scalar(st[b][:, 1:2], st[b][:, 0:1], 1.0 / D,
                                op0=ALU.mult, scalar2=None)
                v.drain()
                v.nop().then_inc(SS["ve"], 1)
            sch.on("vector", v_mean)
            m_tgt = sch.inc("ve", 1)

            def s_var(s, b=b, m_tgt=m_tgt):
                s.wait_ge(SS["ve"], m_tgt)
                s.activation(sq_t[:], xb[b][:], AF.Square, bias=st[b][:, 1:2],
                             scale=-1.0, accum_out=st[b][:, 2:3])
                s.drain()
                s.nop().then_inc(SS["se"], 1)
            sch.on("scalar", s_var)
            v_tgt = sch.inc("se", 1)

            def v_vr(v, b=b, v_tgt=v_tgt):
                v.wait_ge(SS["se"], v_tgt)
                v.tensor_scalar(st[b][:, 2:3], st[b][:, 2:3], 1.0 / D,
                                op0=ALU.mult, scalar2=LN_EPS, op1=ALU.add)
                v.drain()
                v.nop().then_inc(SS["ve"], 1)
            sch.on("vector", v_vr)
            vr_tgt = sch.inc("ve", 1)

            def s_rs(s, b=b, vr_tgt=vr_tgt):
                s.wait_ge(SS["ve"], vr_tgt)
                s.activation(st[b][:, 3:4], st[b][:, 2:3], AF.Ln)
                s.drain()
                s.activation(st[b][:, 4:5], st[b][:, 3:4], AF.Exp, scale=-0.5)
                s.drain()
                s.nop().then_inc(SS["se"], 1)
            sch.on("scalar", s_rs)
            r_tgt = sch.inc("se", 1)

            def v_norm(v, b=b, ssem=ssem, r_tgt=r_tgt, w=sch.get(ssem)):
                v.wait_ge(SS["se"], r_tgt)
                v.wait_ge(SS[ssem], w)
                v.tensor_scalar(lnb[b][:], xb[b][:], st[b][:, 1:2],
                                op0=ALU.subtract, scalar2=st[b][:, 4:5],
                                op1=ALU.mult)
                v.drain()
                v.nop().then_inc(SS["ve"], 1)
            sch.on("vector", v_norm)
            n_tgt = sch.inc("ve", 1)

            def sy_st(sy, i=i, b=b, ssem=ssem, n_tgt=n_tgt):
                sy.wait_ge(SS["ve"], n_tgt)
                sy.dma_start(dst_dram[i * 128:(i + 1) * 128, :], lnb[b][:]
                             ).then_inc(SS[ssem], 16)
            sch.on("sync", sy_st)
            sch.inc(ssem, 16)

    def emit_p1(es_ph, vtm):
        lnfm = [sbt(es_ph, f"lnfm{c}", [128, N], BF16, is_transpose_dest=True)
                for c in range(8)]
        wbuf = [sbt(es_ph, f"p1w{i}", [128, 4 * D], BF16) for i in range(2)]
        stg = [sbt(es_ph, f"p1stg{i}", [128, 8 * 512], BF16) for i in range(2)]
        sqs = sbt(es_ph, "p1sq", [128, 512], BF16)
        qn2 = sbt(es_ph, "qn2", [16, N], BF16)
        kn2 = sbt(es_ph, "kn2", [16, N], BF16)
        pn2 = sbt(es_ph, "pn2", [1, N], BF16)
        kmax = sbt(es_ph, "kmax", [16, 1], F32)
        pa = es_ph.enter_context(nc.psum_tensor("p1a", [128, 512], F32))
        pb = es_ph.enter_context(nc.psum_tensor("p1b", [128, 512], F32))
        pn = es_ph.enter_context(nc.psum_tensor("p1n", [128, 512], F32))
        pt = es_ph.enter_context(nc.psum_tensor("p1t", [128, 2048], BF16))

        s_table(SILUS)

        def v_ones(v):
            v.memset(qn2[:], 0.0)
            v.memset(kn2[:], 0.0)
            v.memset(pn2[:], 0.0)
            v.memset(vtm.rearrange("p (t h o) -> p t h o", t=NJT, h=H, o=65)
                     [:, :, :, 64:65], 1.0)
            v.drain()
            v.nop().then_inc(SS["ve"], 1)
        sch.on("vector", v_ones)
        sch.inc("ve", 1)

        def s_trl(s):
            for c in range(8):
                s.dma_start(lnfm[c][:], ln1_d[:, c * 128:(c + 1) * 128],
                            transpose=True).then_inc(SS["d_trl"], 16)
        sch.on("scalar", s_trl)
        aq_ln = sch.inc("d_trl", 16 * 8)

        weights = [("k", Wk), ("v", Wv), ("q", Wq), ("pre", Wpre)][:P1_NW]
        w_rdy = {}

        def emit_load_w(widx, half, wait_pe):
            def g_ld(g, widx=widx, half=half, wait_pe=wait_pe):
                g.wait_ge(SS["pe"], wait_pe)
                nm, ap = weights[widx]
                g.dma_start(
                    wbuf[half].rearrange("p (c d) -> p c d", c=4),
                    ap.rearrange("(c p) d -> p c d", p=128)[:, half * 4:
                                                           half * 4 + 4, :]
                            ).then_inc(SS[f"d_w{half}"], 16)
            sch.on("gpsimd", g_ld)
            return sch.inc(f"d_w{half}", 16)

        w_rdy[(0, 0)] = emit_load_w(0, 0, 0)
        w_rdy[(0, 1)] = emit_load_w(0, 1, 0)
        NW = len(weights)
        evac_done = {}

        if P1_PART == "skel":
            weights2 = []
        else:
            weights2 = list(enumerate(weights))
        for widx, (nm, wap) in weights2:
            for tk in range(8):
                sg = stg[tk % 2]
                sgsem = f"d_sg{tk % 2}"
                stored = sch.get(sgsem)
                normed = sch.get("se")
                for co in range(8):
                    ps = pa if co % 2 == 0 else pb
                    pkey = id(ps)

                    def t_mm(t, widx=widx, tk=tk, co=co, ps=ps,
                             w0=w_rdy[(widx, 0)], w1=w_rdy[(widx, 1)],
                             ev_w=evac_done.get(id(ps), (0, 0))):
                        if tk == 0 and co == 0:
                            t.wait_ge(SS["d_w0"], w0)
                            t.wait_ge(SS["d_w1"], w1)
                            if widx == 0:
                                t.wait_ge(SS["d_trl"], aq_ln)
                        t.wait_ge(SS["ve"], ev_w[0])
                        t.wait_ge(SS["se"], ev_w[1])
                        for cin in range(8):
                            _mm = t.matmul(ps[:],
                                     wbuf[cin // 4][:, (cin % 4) * D + co * 128:
                                                    (cin % 4) * D + (co + 1) * 128],
                                     lnfm[cin][:, tk * 512:(tk + 1) * 512],
                                     start=(cin == 0), stop=(cin == 7))
                        _mm.then_inc(SS["pe"], 1)
                    sch.on("tensor", t_mm)
                    mm_tgt = sch.inc("pe", 1)

                    if nm == "pre":
                        def s_sil(s, co=co, ps=ps, mm_tgt=mm_tgt, sg=sg,
                                  sgsem=sgsem, stw=stored):
                            s.wait_ge(SS["pe"], mm_tgt)
                            s.wait_ge(SS[sgsem], stw)
                            s.activation(sg[:, co * 512:(co + 1) * 512], ps[:],
                                         AF.Silu, bias=bprec[:, co:co + 1])
                            s.drain()
                            s.nop().then_inc(SS["se"], 1)
                        sch.on("scalar", s_sil)
                        sil_tgt = sch.inc("se", 1)

                        def v_sc(v, co=co, sil_tgt=sil_tgt, sg=sg):
                            v.wait_ge(SS["se"], sil_tgt)
                            v.tensor_scalar(sg[:, co * 512:(co + 1) * 512],
                                            sg[:, co * 512:(co + 1) * 512],
                                            sprev[:, 0:1], op0=ALU.mult,
                                            scalar2=None)
                            v.drain()
                            v.nop().then_inc(SS["ve"], 1)
                        sch.on("vector", v_sc)
                        sch.inc("ve", 1)
                        evac_done[pkey] = (sch.get("ve"), sch.get("se"))
                    else:
                        scale = 0.125 if nm == "q" else 1.0

                        def v_ev(v, co=co, ps=ps, mm_tgt=mm_tgt, scale=scale,
                                 sg=sg, sgsem=sgsem, stw=stored, nw=normed):
                            v.wait_ge(SS["pe"], mm_tgt)
                            v.wait_ge(SS[sgsem], stw)
                            v.wait_ge(SS["se"], nw)
                            v.tensor_scalar(sg[:, co * 512:(co + 1) * 512], ps[:],
                                            scale, op0=ALU.mult, scalar2=None)
                            v.drain()
                            v.nop().then_inc(SS["ve"], 1)
                        sch.on("vector", v_ev)
                        sch.inc("ve", 1)
                        evac_done[pkey] = (sch.get("ve"), sch.get("se"))
                evac_ve = sch.get("ve")
                evac_se = sch.get("se")

                if P1_PART == "mm":
                    continue
                if nm != "v":
                    ddst = {"k": kt_d, "q": qa_d, "pre": pre_d}[nm]

                    def sy_st(sy, tk=tk, ddst=ddst, ve_w=evac_ve, se_w=evac_se,
                              sg=sg, sgsem=sgsem):
                        sy.wait_ge(SS["ve"], ve_w)
                        sy.wait_ge(SS["se"], se_w)
                        sy.dma_start(
                            ddst.rearrange("(c p) n -> p c n", p=128)
                            [:, :, tk * 512:(tk + 1) * 512],
                            sg.rearrange("p (c n) -> p c n", c=8)
                                    ).then_inc(SS[sgsem], 16)
                    sch.on("sync", sy_st)
                    sch.inc(sgsem, 16)
                else:
                    for co in range(8):
                        for j in range(4):
                            jt = tk * 4 + j

                            def t_tr(t, co=co, j=j, ve_w=evac_ve, sg=sg,
                                     evv=max(0, sch.get("ve") - 1)):
                                t.wait_ge(SS["ve"], ve_w)
                                t.wait_ge(SS["ve"], evv)
                                _mm = t.transpose(
                                    pt[:, (j % 2) * 1024:(j % 2) * 1024 + 128],
                                    sg[:, co * 512 + j * 128:
                                       co * 512 + (j + 1) * 128],
                                    ident[:])
                                _mm.then_inc(SS["pe"], 1)
                            sch.on("tensor", t_tr)
                            tr_tgt = sch.inc("pe", 1)

                            def v_tr(v, co=co, jt=jt, j=j, tr_tgt=tr_tgt):
                                v.wait_ge(SS["pe"], tr_tgt)
                                v.tensor_copy(
                                    vtm.rearrange("p (t h o) -> p t h o",
                                                  t=NJT, h=H, o=65)
                                    [:, jt, 2 * co:2 * co + 2, 0:64],
                                    pt[:, (j % 2) * 1024:(j % 2) * 1024 + 128]
                                    .rearrange("p (a b) -> p a b", a=2))
                                v.drain()
                                v.nop().then_inc(SS["ve"], 1)
                            sch.on("vector", v_tr)
                            sch.inc("ve", 1)

                if nm in ("q", "k", "pre") and P1_PART not in ("mm", "mmst"):
                    ndst = {"q": qn2, "k": kn2, "pre": pn2}[nm]
                    for co in range(8):
                        def s_sq(s, co=co, ve_w=evac_ve, se_w=evac_se, sg=sg,
                                 pw=sch.get("pe")):
                            s.wait_ge(SS["ve"], ve_w)
                            s.wait_ge(SS["pe"], pw)
                            s.activation(sqs[:], sg[:, co * 512:(co + 1) * 512],
                                         AF.Square)
                            s.drain()
                            s.nop().then_inc(SS["se"], 1)
                        sch.on("scalar", s_sq)
                        sq_tgt = sch.inc("se", 1)

                        def t_nm(t, nm=nm, co=co, sq_tgt=sq_tgt,
                                 vw=sch.get("ve")):
                            t.wait_ge(SS["se"], sq_tgt)
                            if co == 0:
                                t.wait_ge(SS["ve"], vw)
                            lhs = (onesc[:] if nm == "pre"
                                   else hmall[:, co * 16:(co + 1) * 16])
                            rows = 1 if nm == "pre" else 16
                            _mm = t.matmul(pn[0:rows, 0:512], lhs, sqs[:],
                                     start=(co == 0), stop=(co == 7))
                            _mm.then_inc(SS["pe"], 1)
                        sch.on("tensor", t_nm)
                        sch.inc("pe", 1)

                    def v_nm(v, nm=nm, tk=tk, ndst=ndst, pw=sch.get("pe")):
                        v.wait_ge(SS["pe"], pw)
                        rows = 1 if nm == "pre" else 16
                        v.tensor_copy(ndst[0:rows, tk * 512:(tk + 1) * 512],
                                      pn[0:rows, 0:512])
                        v.drain()
                        v.nop().then_inc(SS["ve"], 1)
                    sch.on("vector", v_nm)
                    sch.inc("ve", 1)
            if widx + 1 < NW:
                w_rdy[(widx + 1, 0)] = emit_load_w(widx + 1, 0, sch.get("pe"))
                w_rdy[(widx + 1, 1)] = emit_load_w(widx + 1, 1, sch.get("pe"))

        if P1_PART in ("skel", "loop"):
            return
        s_table(LNEXP)

        def s_post(s, vw=sch.get("ve")):
            s.wait_ge(SS["ve"], vw)
            s.activation(qn2[:], qn2[:], AF.Ln)
            s.drain()
            s.activation(qn2[:], qn2[:], AF.Exp, scale=0.5)
            s.drain()
            s.activation(kn2[:], kn2[:], AF.Ln)
            s.drain()
            s.activation(kn2[:], kn2[:], AF.Exp, scale=0.5)
            s.activation(pn2[:], pn2[:], AF.Ln)
            s.drain()
            s.activation(pn2[:], pn2[:], AF.Exp, scale=0.5)
            s.drain()
            s.nop().then_inc(SS["se"], 1)
        sch.on("scalar", s_post)
        sq2 = sch.inc("se", 1)

        def v_post(v, sq2=sq2):
            v.wait_ge(SS["se"], sq2)
            v.tensor_reduce(kmax[:], kn2[:], op=ALU.max, axis=AX.X)
            v.tensor_reduce(smA[0:1, 5:6], pn2[:], op=ALU.max, axis=AX.X)
            v.drain()
            v.tensor_scalar(pn2[:], pn2[:], smA[0:1, 5:6], op0=ALU.mult,
                            scalar2=None)
            v.drain()
            v.nop().then_inc(SS["ve"], 1)
        sch.on("vector", v_post)
        pn_tgt = sch.inc("ve", 1)

        for tk in range(8):
            def t_pb(t, tk=tk, pn_tgt=pn_tgt, vw=sch.get("ve")):
                t.wait_ge(SS["ve"], pn_tgt)
                t.wait_ge(SS["ve"], vw)
                _mm = t.matmul(pn[0:16, 0:512], onesrb[0:1, 0:16],
                         pn2[:, tk * 512:(tk + 1) * 512], start=True, stop=True)
                _mm.then_inc(SS["pe"], 1)
            sch.on("tensor", t_pb)
            tgt = sch.inc("pe", 1)

            def v_pb(v, tk=tk, tgt=tgt):
                v.wait_ge(SS["pe"], tgt)
                v.scalar_tensor_tensor(uall[:, tk * 512:(tk + 1) * 512],
                                       qn2[:, tk * 512:(tk + 1) * 512], kmax[:],
                                       pn[0:16, 0:512], op0=ALU.mult,
                                       op1=ALU.add)
                v.drain()
                v.nop().then_inc(SS["ve"], 1)
            sch.on("vector", v_pb)
            sch.inc("ve", 1)

    def emit_p2(es_ph, vtm):
        ktc = [sbt(es_ph, f"kt{c}", [128, N], BF16) for c in range(8)]
        qch = sbt(es_ph, "qch", [128, 8 * IC], BF16)
        strip = sbt(es_ph, "strip", [128, NJT * IC], BF16)
        prej = [sbt(es_ph, f"prej{i}", [128, 8 * 256], BF16) for i in range(2)]
        premov = qch
        ptile = [sbt(es_ph, f"ptile{i}", [128, 2048], BF16) for i in range(2)]
        bneg = sbt(es_ph, "bneg", [128, 16], F32)
        bprojb = sbt(es_ph, "bprojb", [128, D], F32)
        scrF = sbt(es_ph, "scrF", [128, D], F32)
        o65s = [scrF[:, 0:512], scrF[:, 512:1024]]
        x3 = scrF
        uasm_sc = scrF[:, 0:33]
        rcp = sbt(es_ph, "rcp", [128, 8], F32)
        otm = sbt(es_ph, "otm", [128, 2 * 1024], BF16)
        ofm = sbt(es_ph, "ofm", [128, 8 * IC], BF16)
        wpj = sbt(es_ph, "wpj", [128, 8 * 1024], BF16)
        pjs = otm
        ln3 = qch[:, 0:D]
        xm3 = qch[:, D:2 * D]

        def sy_bp(sy):
            sy.dma_start(bprojb[0:1, :], bproj[None, :]).then_inc(SS["d_x"], 16)
        sch.on("sync", sy_bp)
        bp_tgt = sch.inc("d_x", 16)

        with nc.psum_tensor("p2ub", [128, 48], F32) as pub:
            def v_ub(v, vw=sch.get("ve")):
                v.wait_ge(SS["ve"], vw)
                v.tensor_reduce(uasm_sc[0:16, 0:16],
                                uall.rearrange("h (c i) -> h c i", c=NCH),
                                op=ALU.max, axis=AX.X)
                v.drain()
                v.nop().then_inc(SS["ve"], 1)
            sch.on("vector", v_ub)
            ub1 = sch.inc("ve", 1)

            def t_ub(t, ub1=ub1):
                t.wait_ge(SS["ve"], ub1)
                _mm = t.transpose(pub[0:16, 0:16], uasm_sc[0:16, 0:16],
                                  identf[0:16, 0:16])
                _mm.then_inc(SS["pe"], 1)
            sch.on("tensor", t_ub)
            ub2 = sch.inc("pe", 1)

            def v_ub2(v, ub2=ub2):
                v.wait_ge(SS["pe"], ub2)
                v.tensor_reduce(uasm_sc[0:16, 16:17], pub[0:16, 0:16],
                                op=ALU.max, axis=AX.X, negate=True)
                v.drain()
                v.nop().then_inc(SS["ve"], 1)
            sch.on("vector", v_ub2)
            ub3 = sch.inc("ve", 1)

            def t_ub2(t, ub3=ub3):
                t.wait_ge(SS["ve"], ub3)
                _mm = t.transpose(pub[0:1, 16:32], uasm_sc[0:16, 16:17],
                                  identf[0:16, 0:16])
                _mm.then_inc(SS["pe"], 1)
            sch.on("tensor", t_ub2)
            ub4 = sch.inc("pe", 1)

            def v_ub3(v, ub4=ub4):
                v.wait_ge(SS["pe"], ub4)
                v.tensor_copy(uasm_sc[0:1, 17:33], pub[0:1, 16:32])
                v.drain()
                v.nop().then_inc(SS["ve"], 1)
            sch.on("vector", v_ub3)
            ub5 = sch.inc("ve", 1)

            def t_ub3(t, ub5=ub5):
                t.wait_ge(SS["ve"], ub5)
                _mm = t.matmul(pub[:, 32:48], onesr[:], uasm_sc[0:1, 17:33],
                               start=True, stop=True)
                _mm.then_inc(SS["pe"], 1)
            sch.on("tensor", t_ub3)
            ub6 = sch.inc("pe", 1)

            def v_ub4(v, ub6=ub6):
                v.wait_ge(SS["pe"], ub6)
                v.tensor_copy(bneg[:], pub[:, 32:48])
                v.drain()
                v.nop().then_inc(SS["ve"], 1)
            sch.on("vector", v_ub4)
            sch.inc("ve", 1)

        def g_wpj(g):
            g.dma_start(wpj.rearrange("p (c d) -> p c d", c=8),
                        Wproj.rearrange("(c p) d -> p c d", p=128)
                        ).then_inc(SS["d_wp"], 16)
        sch.on("gpsimd", g_wpj)
        wpj_tgt = sch.inc("d_wp", 16)

        def sy_kt(sy):
            for c in range(8):
                sy.dma_start(ktc[c][:], kt_d[c * 128:(c + 1) * 128, :]
                        ).then_inc(SS["d_kt"], 16)
        sch.on("sync", sy_kt)
        kt_tgt = sch.inc("d_kt", 16 * 8)

        with nc.psum_tensor("p2bp", [128, 512], F32) as pbp:
            for half in range(2):
                def t_bp(t, half=half, vw=sch.get("ve"), bp_tgt=bp_tgt):
                    t.wait_ge(SS["ve"], vw)
                    if half == 0:
                        t.wait_ge(SS["d_x"], bp_tgt)
                    for j in range(4):
                        cc = half * 4 + j
                        _mm = t.matmul(pbp[:, j * 128:(j + 1) * 128], onesr[:],
                                 bprojb[0:1, cc * 128:(cc + 1) * 128],
                                 start=True, stop=True)
                    _mm.then_inc(SS["pe"], 1)
                sch.on("tensor", t_bp)
                pet = sch.inc("pe", 1)

                def v_bp(v, half=half, pet=pet):
                    v.wait_ge(SS["pe"], pet)
                    v.tensor_copy(bprojb[:, half * 512:(half + 1) * 512], pbp[:])
                    v.drain()
                    v.nop().then_inc(SS["ve"], 1)
                sch.on("vector", v_bp)
                sch.inc("ve", 1)

        pj_ld_hist = {}
        vfm_hist = {}
        for ch in range(NCH):
            i0 = ch * IC

            def sy_q(sy, ch=ch, i0=i0, pw=sch.get("pe")):
                sy.wait_ge(SS["pe"], pw)
                sy.dma_start(premov.rearrange("p (c n) -> p c n", c=8),
                             pre_d.rearrange("(c p) n -> p c n", p=128)
                             [:, :, i0:i0 + IC]
                            ).then_inc(SS["d_q"], 16)
            sch.on("sync", sy_q)
            q_tgt = sch.inc("d_q", 16)

            with nc.psum_tensor(f"p2sA{ch}", [128, 512], F32) as pstA, \
                 nc.psum_tensor(f"p2sB{ch}", [128, 512], F32) as pstB:
                pst2 = [pstA, pstB]
                ev_hist = {}
                st_hist = {}
                ch_ve0 = vfm_hist.get(ch - 1, sch.get("ve") if ch == 0 else 0)
                if ch > 0 and ch_ve0 == 0:
                    ch_ve0 = sch.get("ve")
                for ld in range(16):
                    pj = prej[ld % 2]
                    pjsem = f"d_pj{ld % 2}"
                    qeng = "sync" if ld % 2 == 0 else "gpsimd"
                    ldw = st_hist.get((ch, ld - 2), pj_ld_hist.get(ch - 1, 0)
                                      if ld < 2 else 0)

                    def q_pj(e, ld=ld, pj=pj, pjsem=pjsem, ldw=ldw):
                        e.wait_ge(SS["pe"], ldw)
                        e.dma_start(pj.rearrange("p (c n) -> p c n", c=8),
                                    pre_d.rearrange("(c p) n -> p c n", p=128)
                                    [:, :, ld * 256:(ld + 1) * 256]
                                    ).then_inc(SS[pjsem], 16)
                    sch.on(qeng, q_pj)
                    pj_tgt = sch.inc(pjsem, 16)

                    for sub in range(2):
                        jt = ld * 2 + sub
                        pst = pst2[jt % 2]

                        def t_st(t, sub=sub, pj=pj, pjsem=pjsem,
                                 pj_tgt=pj_tgt, q_tgt=q_tgt, pst=pst,
                                 vw=ev_hist.get(jt - 2, ch_ve0)):
                            t.wait_ge(SS[pjsem], pj_tgt)
                            t.wait_ge(SS["d_q"], q_tgt)
                            t.wait_ge(SS["d_kt"], kt_tgt)
                            t.wait_ge(SS["ve"], vw)
                            for cc in range(8):
                                _mm = t.matmul(
                                    pst[:, 0:IC],
                                    pj[:, cc * 256 + sub * 128:
                                       cc * 256 + (sub + 1) * 128],
                                    premov[:, cc * IC:(cc + 1) * IC],
                                    start=(cc == 0), stop=(cc == 7))
                            _mm.then_inc(SS["pe"], 1)
                        sch.on("tensor", t_st)
                        smm = sch.inc("pe", 1)
                        st_hist[(ch, ld)] = smm

                        def v_st(v, jt=jt, smm=smm, pst=pst):
                            v.wait_ge(SS["pe"], smm)
                            v.tensor_copy(strip[:, jt * IC:(jt + 1) * IC],
                                          pst[:, 0:IC])
                            v.drain()
                            v.nop().then_inc(SS["ve"], 1)
                        sch.on("vector", v_st)
                        ev_hist[jt] = sch.inc("ve", 1)
                pj_ld_hist[ch] = sch.get("pe")

            def sy_q2(sy, ch=ch, i0=i0, pw=sch.get("pe")):
                sy.wait_ge(SS["pe"], pw)
                sy.dma_start(qch.rearrange("p (c n) -> p c n", c=8),
                             qa_d.rearrange("(c p) n -> p c n", p=128)
                             [:, :, i0:i0 + IC]).then_inc(SS["d_q"], 16)
            sch.on("sync", sy_q2)
            q2_tgt = sch.inc("d_q", 16)


            strip_rdy = sch.get("ve")
            for sg in range(2):
                sg_ve = sch.get("ve")
                es_ob = ExitStack()
                es_sc = ExitStack()
                ob = [es_ob.enter_context(
                    nc.psum_tensor(f"ob{i}_{ch}_{sg}", [128, 512], F32))
                    for i in range(4)]
                scq = [es_sc.enter_context(
                    nc.psum_tensor(f"sc{q}_{ch}_{sg}", [128, 1024], F32))
                    for q in range(2)]
                exp_tgt = [[None] * NJT for _ in range(2)]
                av_tgt = [[None] * NJT for _ in range(2)]

                def mk_sc(jt, qd, sw, vw0, sg=sg, scq=scq, q2=q2_tgt):
                    def t_sc(t, jt=jt, qd=qd, sw=sw, vw0=vw0, sg=sg, scq=scq,
                             q2=q2):
                        if sw is not None:
                            t.wait_ge(SS["se"], sw)
                        if vw0 is not None:
                            t.wait_ge(SS["ve"], vw0)
                            t.wait_ge(SS["d_q"], q2)
                        st_ap = strip[:, jt * IC:(jt + 1) * IC]
                        h0 = sg * 8 + qd * 4
                        for r in range(4):
                            _mm = t.matmul(
                                scq[qd][:, r * IC:(r + 1) * IC], sid[:],
                                st_ap, start=(r % 2 == 0), stop=False,
                                skip_group_check=True)
                        order = [(h0, 0), (h0 + 1, 2), (h0 + 2, 1),
                                 (h0 + 3, 3)]
                        for hh, reg in order:
                            cc = hh // 2
                            hb = (hh % 2) * 64
                            _mm = t.matmul(
                                scq[qd][:, reg * IC:(reg + 1) * IC],
                                ktc[cc][hb:hb + 64,
                                        jt * 128:(jt + 1) * 128],
                                qch[hb:hb + 64, cc * IC:(cc + 1) * IC],
                                start=False, stop=(reg >= 2),
                                skip_group_check=True)
                        _mm.then_inc(SS["pe"], 1)
                    sch.on("tensor", t_sc)
                    return sch.inc("pe", 1)

                def mk_ex(jt, qd, sc_t, av_w, scq=scq):
                    def s_ex(s, jt=jt, qd=qd, sc_t=sc_t, av_w=av_w, scq=scq):
                        s.wait_ge(SS["pe"], sc_t)
                        if av_w is not None:
                            s.wait_ge(SS["pe"], av_w)
                        s.activation(ptile[jt % 2][:, qd * 1024:(qd + 1) * 1024],
                                     scq[qd][:], AF.Exp, bias=bneg[:, ch:ch + 1])
                        s.drain()
                        s.nop().then_inc(SS["se"], 1)
                    sch.on("scalar", s_ex)
                    return sch.inc("se", 1)

                def mk_av(jt, qd, ex_t, sg=sg, ob=ob):
                    def t_av(t, jt=jt, qd=qd, ex_t=ex_t, sg=sg, ob=ob):
                        t.wait_ge(SS["se"], ex_t)
                        h0 = sg * 8 + qd * 4
                        order = [(h0, 0), (h0 + 2, 1), (h0 + 1, 2),
                                 (h0 + 3, 3)]
                        for hh, reg in order:
                            obk = ob[qd * 2 + (0 if reg < 2 else 1)]
                            col = reg % 2
                            _mm = t.matmul(
                                obk[0:65, col * IC:(col + 1) * IC],
                                vtm.rearrange("p (t h o) -> p t h o",
                                              t=NJT, h=H, o=65)[:, jt, hh, :],
                                ptile[jt % 2][:, qd * 1024 + reg * IC:
                                              qd * 1024 + (reg + 1) * IC],
                                start=(jt == 0 and col == 0),
                                stop=(jt == NJT - 1),
                                skip_group_check=True)
                        _mm.then_inc(SS["pe"], 1)
                    sch.on("tensor", t_av)
                    return sch.inc("pe", 1)

                for jt in range(NJT):
                    for qd in range(2):
                        sw = exp_tgt[qd][jt - 1] if jt >= 1 else None
                        vw0 = max(strip_rdy, sg_ve) if jt == 0 else None
                        sc_t = mk_sc(jt, qd, sw, vw0)
                        av_w = av_tgt[qd][jt - 2] if jt >= 2 else None
                        exp_tgt[qd][jt] = mk_ex(jt, qd, sc_t, av_w)
                    if jt >= 1:
                        for qd in range(2):
                            av_tgt[qd][jt - 1] = mk_av(jt - 1, qd,
                                                       exp_tgt[qd][jt - 1])
                for qd in range(2):
                    av_tgt[qd][NJT - 1] = mk_av(NJT - 1, qd,
                                                exp_tgt[qd][NJT - 1])

                es_sc.close()
                attnv_done = sch.get("pe")
                with nc.psum_tensor(f"ptrA_{ch}_{sg}", [128, 512], F32) as ptrA, \
                     nc.psum_tensor(f"ptrB_{ch}_{sg}", [128, 512], F32) as ptrB:
                    ptr2 = [ptrA, ptrB]
                    oe_hist = {}
                    on_hist = {}
                    for ob_i in range(4):
                        qd = ob_i // 2
                        h0 = sg * 8 + qd * 4
                        heads = ((h0, h0 + 2) if ob_i % 2 == 0
                                 else (h0 + 1, h0 + 3))
                        o65 = o65s[ob_i % 2]
                        ptr = ptr2[ob_i % 2]

                        def s_oe(s, obk=ob[ob_i], o65=o65,
                                 attnv_done=attnv_done,
                                 pw=oe_hist.get(ob_i - 2, 0),
                                 vw=sch.get("ve") if ob_i < 2 else 0):
                            s.wait_ge(SS["pe"], attnv_done)
                            if pw:
                                s.wait_ge(SS["pe"], pw)
                            if vw:
                                s.wait_ge(SS["ve"], vw)
                            s.copy(o65[0:65, :], obk[0:65, :])
                            s.drain()
                            s.nop().then_inc(SS["se"], 1)
                        sch.on("scalar", s_oe)
                        cp_tgt = sch.inc("se", 1)

                        def t_otr(t, cp_tgt=cp_tgt, ptr=ptr, o65=o65,
                                  vw=on_hist.get(ob_i - 2, 0)):
                            t.wait_ge(SS["se"], cp_tgt)
                            if vw:
                                t.wait_ge(SS["ve"], vw)
                            for bb in range(4):
                                _mm = t.transpose(ptr[:, bb * 65:(bb + 1) * 65],
                                                  o65[0:65,
                                                      bb * 128:(bb + 1) * 128],
                                                  identf[0:65, 0:65])
                            _mm.then_inc(SS["pe"], 1)
                        sch.on("tensor", t_otr)
                        oe_hist[ob_i] = tr_tgt = sch.inc("pe", 1)

                        def v_on(v, heads=heads, tr_tgt=tr_tgt, ptr=ptr,
                                 ob_i=ob_i):
                            v.wait_ge(SS["pe"], tr_tgt)
                            for bb in range(4):
                                v.reciprocal(rcp[:, (ob_i % 2) * 4 + bb:
                                                 (ob_i % 2) * 4 + bb + 1],
                                             ptr[:, bb * 65 + 64:bb * 65 + 65])
                            v.drain()
                            for bb in range(4):
                                hh = heads[bb // 2]
                                ih = bb % 2
                                v.tensor_scalar(
                                    otm[:, ih * 1024 + hh * 64:
                                        ih * 1024 + (hh + 1) * 64],
                                    ptr[:, bb * 65:bb * 65 + 64],
                                    rcp[:, (ob_i % 2) * 4 + bb:
                                        (ob_i % 2) * 4 + bb + 1],
                                    op0=ALU.mult, scalar2=None)
                            v.drain()
                            v.nop().then_inc(SS["ve"], 1)
                        sch.on("vector", v_on)
                        on_hist[ob_i] = sch.inc("ve", 1)
                es_ob.close()

            with nc.psum_tensor(f"pfm{ch}", [128, 2048], BF16) as pfm, \
                 nc.psum_tensor(f"ppjA{ch}", [128, 512], F32) as ppjA, \
                 nc.psum_tensor(f"ppjB{ch}", [128, 512], F32) as ppjB, \
                 nc.psum_tensor(f"ptm{ch}", [128, 1024], BF16) as ptm:
                ppj2 = [ppjA, ppjB]
                pj_hist = {}
                for ih in range(2):
                    for cc in range(8):
                        def t_fm(t, ih=ih, cc=cc, pfm=pfm, vw=sch.get("ve"),
                                 evv=max(0, sch.get("ve") - 1)):
                            t.wait_ge(SS["ve"], vw if (ih == 0 and cc == 0)
                                      else evv)
                            _mm = t.transpose(
                                pfm[:, (cc % 2) * 1024:(cc % 2) * 1024 + 128],
                                otm[:, ih * 1024 + cc * 128:
                                    ih * 1024 + (cc + 1) * 128],
                                ident[:])
                            _mm.then_inc(SS["pe"], 1)
                        sch.on("tensor", t_fm)
                        trt = sch.inc("pe", 1)

                        def v_fm(v, ih=ih, cc=cc, trt=trt, pfm=pfm):
                            v.wait_ge(SS["pe"], trt)
                            v.tensor_copy(
                                ofm[:, cc * IC + ih * 128:
                                    cc * IC + (ih + 1) * 128],
                                pfm[:, (cc % 2) * 1024:(cc % 2) * 1024 + 128])
                            v.drain()
                            v.nop().then_inc(SS["ve"], 1)
                        sch.on("vector", v_fm)
                        vfm_hist[ch] = sch.inc("ve", 1)

                for co in range(8):
                    ppj = ppj2[co % 2]

                    def t_pj(t, co=co, ppj=ppj, wt=wpj_tgt,
                             vw=sch.get("ve"),
                             evw=pj_hist.get(co - 2, 0)):
                        if co == 0:
                            t.wait_ge(SS["d_wp"], wt)
                        t.wait_ge(SS["ve"], vw if co == 0 else evw)
                        for cin in range(8):
                            _mm = t.matmul(
                                ppj[:, 0:IC],
                                wpj[:, cin * 1024 + co * 128:
                                    cin * 1024 + (co + 1) * 128],
                                ofm[:, cin * IC:(cin + 1) * IC],
                                start=(cin == 0), stop=(cin == 7))
                        _mm.then_inc(SS["pe"], 1)
                    sch.on("tensor", t_pj)
                    pj_tgt = sch.inc("pe", 1)

                    def v_pj(v, co=co, pj_tgt=pj_tgt, ppj=ppj):
                        v.wait_ge(SS["pe"], pj_tgt)
                        v.tensor_copy(pjs[:, co * IC:(co + 1) * IC], ppj[:, 0:IC])
                        v.drain()
                        v.nop().then_inc(SS["ve"], 1)
                    sch.on("vector", v_pj)
                    pj_hist[co] = sch.inc("ve", 1)

                for ih in range(2):
                    def sy_x(sy, i0=i0, ih=ih, vw=sch.get("ve")):
                        sy.wait_ge(SS["ve"], vw)
                        sy.dma_start(x3[:],
                                     x_in[i0 + ih * 128:i0 + (ih + 1) * 128, :]
                                     ).then_inc(SS["d_x"], 16)
                        sy.dma_start(ln3[:],
                                     ln1_d[i0 + ih * 128:i0 + (ih + 1) * 128, :]
                                    ).then_inc(SS["d_x"], 16)
                    sch.on("sync", sy_x)
                    x_tgt = sch.inc("d_x", 32)

                    def t_tm(t, ih=ih, ptm=ptm, vw=sch.get("ve")):
                        t.wait_ge(SS["ve"], vw)
                        for co in range(8):
                            _mm = t.transpose(ptm[:, co * 128:(co + 1) * 128],
                                        pjs[:, co * IC + ih * 128:
                                            co * IC + (ih + 1) * 128],
                                        ident[:])
                        _mm.then_inc(SS["pe"], 1)
                    sch.on("tensor", t_tm)
                    tmt = sch.inc("pe", 1)

                    def v_cmb(v, ih=ih, tmt=tmt, x_tgt=x_tgt, ptm=ptm,
                              xmw=sch.get("d_xm")):
                        v.wait_ge(SS["pe"], tmt)
                        v.wait_ge(SS["d_x"], x_tgt)
                        v.wait_ge(SS["d_xm"], xmw)
                        v.scalar_tensor_tensor(xm3[:], ptm[:], 1.0, bprojb[:],
                                               op0=ALU.mult, op1=ALU.add)
                        v.drain()
                        v.tensor_tensor(xm3[:], xm3[:], ln3[:], op=ALU.add)
                        v.drain()
                        v.tensor_tensor(xm3[:], xm3[:], x3[:], op=ALU.add)
                        v.drain()
                        v.nop().then_inc(SS["ve"], 1)
                    sch.on("vector", v_cmb)
                    cm_tgt = sch.inc("ve", 1)

                    def sy_xm(sy, i0=i0, ih=ih, cm_tgt=cm_tgt):
                        sy.wait_ge(SS["ve"], cm_tgt)
                        sy.dma_start(
                            xmid_d[i0 + ih * 128:i0 + (ih + 1) * 128, :], xm3[:]
                                    ).then_inc(SS["d_xm"], 16)
                    sch.on("sync", sy_xm)
                    sch.inc("d_xm", 16)

    def emit_f0(es_ph, ln2fm):
        ggw = sbt(es_ph, "ggw", [128, 8 * 32], BF16)
        lgfm = sbt(es_ph, "lgfm", [32, N], F32)
        lg_t = sbt(es_ph, "lg_t", [128, 32], F32)
        mx = sbt(es_ph, "mx", [128, 8], F32)
        ohf = sbt(es_ph, "ohf", [128, 32], F32)
        ohb = sbt(es_ph, "ohb", [128, 32], BF16)
        wnf = sbt(es_ph, "wnf", [128, 32], F32)
        cntc = sbt(es_ph, "cntc", [32, 1], F32)
        runrow = sbt(es_ph, "runrow", [1, 32], F32)
        bcr = sbt(es_ph, "bcr", [128, 32], F32)
        tmp32 = sbt(es_ph, "tmp32", [128, 32], F32)
        pv = sbt(es_ph, "pv", [128, 8], F32)
        ln2t = sbt(es_ph, "ln2t", [128, D], BF16)
        xst = sbt(es_ph, "xst", [128, D], BF16)

        def s_trl(s):
            for c in range(8):
                s.dma_start(ln2fm[c][:], ln2_d[:, c * 128:(c + 1) * 128],
                            transpose=True).then_inc(SS["d_trl"], 16)
        sch.on("scalar", s_trl)
        aq_ln2 = sch.inc("d_trl", 16 * 8)

        def g_gw(g):
            g.dma_start(ggw.rearrange("p (c e) -> p c e", c=8),
                        gate_w.rearrange("(c p) e -> p c e", p=128)
                        ).then_inc(SS["d_wp"], 16)
        sch.on("gpsimd", g_gw)
        gw_tgt = sch.inc("d_wp", 16)

        plg = es_ph.enter_context(nc.psum_tensor("plg", [32, 512], F32))
        prt = es_ph.enter_context(nc.psum_tensor("prt", [128, 256], F32))
        for tk in range(8):
            def t_lg(t, tk=tk, gw_tgt=gw_tgt, aq_ln2=aq_ln2,
                     vw=sch.get("ve")):
                if tk == 0:
                    t.wait_ge(SS["d_wp"], gw_tgt)
                    t.wait_ge(SS["d_trl"], aq_ln2)
                t.wait_ge(SS["ve"], vw)
                for cin in range(8):
                    _mm = t.matmul(plg[:], ggw[:, cin * 32:(cin + 1) * 32],
                             ln2fm[cin][:, tk * 512:(tk + 1) * 512],
                             start=(cin == 0), stop=(cin == 7))
                _mm.then_inc(SS["pe"], 1)
            sch.on("tensor", t_lg)
            lg_tgt = sch.inc("pe", 1)

            def v_lg(v, tk=tk, lg_tgt=lg_tgt):
                v.wait_ge(SS["pe"], lg_tgt)
                v.tensor_copy(lgfm[:, tk * 512:(tk + 1) * 512], plg[:])
                v.drain()
                v.nop().then_inc(SS["ve"], 1)
            sch.on("vector", v_lg)
            sch.inc("ve", 1)

        def v_run0(v):
            v.memset(runrow[:], 0.0)
            v.drain()
            v.nop().then_inc(SS["ve"], 1)
        sch.on("vector", v_run0)
        sch.inc("ve", 1)

        for tt in range(NTT):
            def t_lgt(t, tt=tt, vw=sch.get("ve")):
                t.wait_ge(SS["ve"], vw)
                _mm = t.transpose(prt[:, 0:32], lgfm[:, tt * 128:(tt + 1) * 128],
                            identf[0:32, 0:32])
                _mm.then_inc(SS["pe"], 1)
            sch.on("tensor", t_lgt)
            tr_tgt = sch.inc("pe", 1)

            def v_rt(v, tt=tt, tr_tgt=tr_tgt):
                v.wait_ge(SS["pe"], tr_tgt)
                v.tensor_copy(lg_t[:], prt[:, 0:32])
                v.drain()
                v.tensor_reduce(mx[:, 0:1], lg_t[:], op=ALU.max, axis=AX.X)
                v.drain()
                v.tensor_scalar(ohf[:], lg_t[:], mx[:, 0:1], op0=ALU.is_ge,
                                scalar2=None)
                v.drain()
                v.tensor_scalar(tmp32[:], io32[:], 65536.0, op0=ALU.subtract,
                                scalar2=None)
                v.drain()
                v.tensor_tensor(tmp32[:], ohf[:], tmp32[:], op=ALU.mult)
                v.drain()
                v.tensor_reduce(mx[:, 1:2], tmp32[:], op=ALU.min, axis=AX.X)
                v.drain()
                v.tensor_scalar(mx[:, 1:2], mx[:, 1:2], 65536.0, op0=ALU.add,
                                scalar2=None)
                v.drain()
                v.tensor_scalar(ohb[:], io32[:], mx[:, 1:2], op0=ALU.is_equal,
                                scalar2=None)
                v.drain()
                v.nop().then_inc(SS["ve"], 1)
            sch.on("vector", v_rt)
            rt_tgt = sch.inc("ve", 1)

            def s_sig(s, rt_tgt=rt_tgt):
                s.wait_ge(SS["ve"], rt_tgt)
                s.activation(mx[:, 3:4], mx[:, 0:1], AF.Exp, scale=-1.0)
                s.drain()
                s.nop().then_inc(SS["se"], 1)
            sch.on("scalar", s_sig)
            sg_tgt = sch.inc("se", 1)

            def v_sig(v, tt=tt, sg_tgt=sg_tgt):
                v.wait_ge(SS["se"], sg_tgt)
                v.tensor_scalar(mx[:, 4:5], mx[:, 3:4], 1.0, op0=ALU.add,
                                scalar2=None)
                v.drain()
                v.reciprocal(sc_all[:, tt:tt + 1], mx[:, 4:5])
                v.drain()
                v.nop().then_inc(SS["ve"], 1)
            sch.on("vector", v_sig)
            sig_tgt = sch.inc("ve", 1)

            def t_cs(t, sig_tgt=sig_tgt):
                t.wait_ge(SS["ve"], sig_tgt)
                _mm = t.matmul(prt[:, 32:64], ut128[:], ohb[:], start=True, stop=True,
                         skip_group_check=True)
                _mm.then_inc(SS["pe"], 1)
            sch.on("tensor", t_cs)
            cs_tgt = sch.inc("pe", 1)

            def v_cs(v, cs_tgt=cs_tgt):
                v.wait_ge(SS["pe"], cs_tgt)
                v.tensor_copy(wnf[:], prt[:, 32:64])
                v.drain()
                v.nop().then_inc(SS["ve"], 1)
            sch.on("vector", v_cs)
            wn_tgt = sch.inc("ve", 1)

            def t_cn(t, wn_tgt=wn_tgt):
                t.wait_ge(SS["ve"], wn_tgt)
                _mm = t.transpose(prt[0:32, 64:192], wnf[:], identf[:])
                _mm.then_inc(SS["pe"], 1)
            sch.on("tensor", t_cn)
            cn_tgt = sch.inc("pe", 1)

            def v_cn(v, cn_tgt=cn_tgt):
                v.wait_ge(SS["pe"], cn_tgt)
                v.tensor_copy(cntc[:], prt[0:32, 191:192])
                v.drain()
                v.nop().then_inc(SS["ve"], 1)
            sch.on("vector", v_cn)
            cc_tgt = sch.inc("ve", 1)

            def t_cr(t, cc_tgt=cc_tgt):
                t.wait_ge(SS["ve"], cc_tgt)
                _mm = t.transpose(prt[0:1, 192:224], cntc[:], identf[0:32, 0:32])
                _mm = t.matmul(prt[:, 224:256], onesr[:], runrow[:],
                         start=True, stop=True, skip_group_check=True)
                _mm.then_inc(SS["pe"], 1)
            sch.on("tensor", t_cr)
            cr_tgt = sch.inc("pe", 1)

            def v_pv(v, tt=tt, cr_tgt=cr_tgt):
                v.wait_ge(SS["pe"], cr_tgt)
                v.scalar_tensor_tensor(bcr[:], prt[:, 224:256], 1.0, cem1[:],
                                       op0=ALU.mult, op1=ALU.add)
                v.drain()
                v.scalar_tensor_tensor(tmp32[:], prt[:, 32:64], 1.0, bcr[:],
                                       op0=ALU.mult, op1=ALU.add)
                v.drain()
                v.tensor_tensor(bcr[:], tmp32[:], ohb[:], op=ALU.mult)
                v.drain()
                v.tensor_reduce(pv[:, 0:1], bcr[:], op=ALU.add, axis=AX.X)
                v.drain()
                v.tensor_scalar(pv[:, 0:1], pv[:, 0:1], -1.0, op0=ALU.add,
                                scalar2=None)
                v.drain()
                v.tensor_tensor(bcr[:], io32[:], ohb[:], op=ALU.mult)
                v.drain()
                v.tensor_reduce(pv[:, 1:2], bcr[:], op=ALU.add, axis=AX.X)
                v.drain()
                v.tensor_scalar(pv[:, 2:3], pv[:, 1:2], -float(CAP),
                                op0=ALU.mult, scalar2=pv[:, 0:1], op1=ALU.add)
                v.drain()
                v.tensor_scalar(pv[:, 3:4], pv[:, 2:3], float(CAP) - 0.5,
                                op0=ALU.is_ge, scalar2=None)
                v.drain()
                v.tensor_tensor(pv[:, 4:5], tloc[:], pv[:, 0:1],
                                op=ALU.subtract)
                v.drain()
                v.tensor_scalar(pv[:, 5:6], pv[:, 4:5], pv[:, 3:4],
                                op0=ALU.mult, scalar2=pv[:, 0:1], op1=ALU.add)
                v.drain()
                v.tensor_copy(p_all[:, tt:tt + 1], pv[:, 5:6])
                v.drain()
                v.tensor_tensor(runrow[:], runrow[:], prt[0:1, 192:224],
                                op=ALU.add)
                v.drain()
                v.nop().then_inc(SS["ve"], 1)
            sch.on("vector", v_pv)
            pv_tgt = sch.inc("ve", 1)

            def sy_l2(sy, tt=tt, gw=sch.get("d_sct")):
                sy.wait_ge(SS["d_sct"], gw)
                sy.dma_start(ln2t[:], ln2_d[tt * 128:(tt + 1) * 128, :]
                            ).then_inc(SS["d_l2"], 16)
            sch.on("sync", sy_l2)
            l2_tgt = sch.inc("d_l2", 16)

            def v_xs(v, tt=tt, l2_tgt=l2_tgt, pv_tgt=pv_tgt):
                v.wait_ge(SS["d_l2"], l2_tgt)
                v.wait_ge(SS["ve"], pv_tgt)
                v.tensor_scalar(xst[:], ln2t[:], sc_all[:, tt:tt + 1],
                                op0=ALU.mult, scalar2=None)
                v.drain()
                v.nop().then_inc(SS["ve"], 1)
            sch.on("vector", v_xs)
            xs_tgt = sch.inc("ve", 1)

            def g_sc(g, tt=tt, xs_tgt=xs_tgt, zw=gq_init_tgt[0]):
                g.wait_ge(SS["ve"], xs_tgt)
                if tt == 0:
                    g.wait_ge(SS["d_ini"], zw)
                g.indirect_dma_start(
                    out=sorted_d,
                    out_offset=IndirectOffsetOnAxis(ap=p_all[:, tt:tt + 1],
                                                    axis=0),
                    in_=xst[:], in_offset=None).then_inc(SS["d_sct"], 16)
            sch.on("gpsimd", g_sc)
            sch.inc("d_sct", 16)

    def emit_f2(es_ph, ln2fm):
        DEPS = {"g": 16, "u": 16, "d": 16}
        sxt = [sbt(es_ph, f"sxt{i}", [128, 8 * 256], BF16,
                   is_transpose_dest=True) for i in range(2)]
        gbuf = [sbt(es_ph, f"gbuf{i}", [128, IFF], BF16)
                for i in range(DEPS["g"])]
        ubuf = [sbt(es_ph, f"ubuf{i}", [128, IFF], BF16)
                for i in range(DEPS["u"])]
        dbuf = [sbt(es_ph, f"dbuf{i}", [128, D], BF16)
                for i in range(DEPS["d"])]
        gsil = sbt(es_ph, "gsil", [128, IFF], BF16)
        actm = sbt(es_ph, "actm", [128, 2 * IFF], BF16)
        actf = sbt(es_ph, "actf", [128, 16 * 256], BF16)
        dst = sbt(es_ph, "dst", [128, D], BF16)

        s_table(SILUS)
        scatter_gq = sch.get("d_sct")
        fams = {"g": gbuf, "u": ubuf, "d": dbuf}
        pending = {f: [] for f in fams}
        emitted = {f: 0 for f in fams}
        consumed = {f: 0 for f in fams}
        ldinfo = {f: {} for f in fams}
        cons = {f: {} for f in fams}

        def pump(fam):
            DEP = DEPS[fam]
            while (pending[fam] and
                   emitted[fam] < consumed[fam] + DEP):
                n = emitted[fam]
                slot = n % DEP
                w = cons[fam].get(n - DEP, 0)
                src_ap = pending[fam].pop(0)
                bufs = fams[fam]

                def g_ld(g, fam=fam, slot=slot, w=w, src_ap=src_ap,
                         bufs=bufs):
                    g.wait_ge(SS["pe"], w)
                    g.dma_start(bufs[slot][:], src_ap
                                ).then_inc(SS[f"d_{fam}{slot}"], 16)
                sch.on("gpsimd", g_ld)
                ldinfo[fam][n] = (slot, sch.inc(f"d_{fam}{slot}", 16))
                emitted[fam] += 1

        def consume(fam, pe_target):
            cons[fam][consumed[fam]] = pe_target
            consumed[fam] += 1
            for f in ("g", "u", "d"):
                pump(f)

        sx_rdy = [0, 0]

        def emit_sxt(e, slot):
            def s_sx(s, e=e, slot=slot, pw=sch.get("pe"),
                     scatter_gq=scatter_gq):
                s.wait_ge(SS["pe"], pw)
                if e == 0:
                    s.wait_ge(SS["d_sct"], scatter_gq)
                for cc in range(8):
                    s.dma_start(sxt[slot][:, cc * 256:(cc + 1) * 256],
                                sorted_d[e * CAP:(e + 1) * CAP,
                                         cc * 128:(cc + 1) * 128],
                                transpose=True).then_inc(SS[f"d_sx{slot}"], 16)
            sch.on("sync", s_sx)
            return sch.inc(f"d_sx{slot}", 16 * 8)

        sx_rdy[0] = emit_sxt(0, 0)

        for e in range(E):
            slot = e % 2
            if e + 1 < E:
                sx_rdy[(e + 1) % 2] = emit_sxt(e + 1, (e + 1) % 2)
            for c in range(8):
                pending["g"].append(r_g[e, c * 128:(c + 1) * 128, :])
                pending["u"].append(r_u[e, c * 128:(c + 1) * 128, :])
            for c in range(16):
                pending["d"].append(r_d[e, c * 128:(c + 1) * 128, :])
            for f in ("g", "u", "d"):
                pump(f)

            es_gu = ExitStack()
            pgu = es_gu.enter_context(
                nc.psum_tensor(f"pgu{e}", [128, 2 * IFF], F32))
            sil_tgt = {}
            act_tgt_h = {}
            for h in range(2):
                for cin in range(8):
                    lslot, ltgt = ldinfo["g"][e * 8 + cin]

                    def t_g(t, h=h, cin=cin, lslot=lslot, ltgt=ltgt, pgu=pgu,
                            sx_t=sx_rdy[slot], slot=slot,
                            sw=sil_tgt.get(("s", h - 1), 0),
                            vw=sch.get("ve")):
                        t.wait_ge(SS[f"d_g{lslot}"], ltgt)
                        if cin == 0:
                            if h == 0:
                                t.wait_ge(SS[f"d_sx{slot}"], sx_t)
                                t.wait_ge(SS["ve"], vw)
                            else:
                                t.wait_ge(SS["se"], sw)
                        for pt_i in range(2):
                            for nb in range(2):
                                _mm = t.matmul(
                                    pgu[:, (pt_i * 2 + nb) * 512:
                                        (pt_i * 2 + nb + 1) * 512],
                                    sxt[slot][:, cin * 256 + pt_i * 128:
                                              cin * 256 + (pt_i + 1) * 128],
                                    gbuf[lslot][:, h * 1024 + nb * 512:
                                                h * 1024 + (nb + 1) * 512],
                                    start=(cin == 0), stop=(cin == 7))
                        _mm.then_inc(SS["pe"], 1)
                    sch.on("tensor", t_g)
                    tgt = sch.inc("pe", 1)
                    if h == 1:
                        consume("g", tgt)
                g_done = sch.get("pe")

                for cin in range(8):
                    lslot, ltgt = ldinfo["u"][e * 8 + cin]

                    def t_u(t, h=h, cin=cin, lslot=lslot, ltgt=ltgt, slot=slot,
                            pgu=pgu, aw=act_tgt_h.get(h - 1, 0)):
                        t.wait_ge(SS[f"d_u{lslot}"], ltgt)
                        if cin == 0 and h == 1:
                            t.wait_ge(SS["ve"], aw)
                        for pt_i in range(2):
                            for nb in range(2):
                                _mm = t.matmul(
                                    pgu[:, IFF + (pt_i * 2 + nb) * 512:
                                        IFF + (pt_i * 2 + nb + 1) * 512],
                                    sxt[slot][:, cin * 256 + pt_i * 128:
                                              cin * 256 + (pt_i + 1) * 128],
                                    ubuf[lslot][:, h * 1024 + nb * 512:
                                                h * 1024 + (nb + 1) * 512],
                                    start=(cin == 0), stop=(cin == 7))
                        _mm.then_inc(SS["pe"], 1)
                    sch.on("tensor", t_u)
                    tgt = sch.inc("pe", 1)
                    if h == 1:
                        consume("u", tgt)
                u_done = sch.get("pe")

                def s_sil(s, h=h, g_done=g_done, pgu=pgu,
                          vw=sch.get("ve")):
                    s.wait_ge(SS["pe"], g_done)
                    s.wait_ge(SS["ve"], vw)
                    s.activation(gsil[:, 0:IFF], pgu[:, 0:IFF], AF.Silu)
                    s.drain()
                    s.nop().then_inc(SS["se"], 1)
                sch.on("scalar", s_sil)
                sil_tgt[("s", h)] = sch.inc("se", 1)

                def v_act(v, h=h, pgu=pgu, st=sil_tgt[("s", h)],
                          uw=u_done):
                    v.wait_ge(SS["se"], st)
                    v.wait_ge(SS["pe"], uw)
                    v.tensor_tensor(
                        actm.rearrange("p (a i) -> p a i", a=2)
                        [:, :, h * 1024:(h + 1) * 1024],
                        gsil.rearrange("p (a i) -> p a i", a=2)
                        [:, :, 0:1024] if False else
                        gsil[:, 0:IFF].rearrange("p (a i) -> p a i", a=2),
                        pgu[:, IFF:2 * IFF].rearrange("p (a i) -> p a i", a=2),
                        op=ALU.mult)
                    v.drain()
                    v.nop().then_inc(SS["ve"], 1)
                sch.on("vector", v_act)
                act_tgt_h[h] = sch.inc("ve", 1)
            act_tgt = act_tgt_h[1]
            if F2DBG and e == 0:
                def sy_dbg1(sy, act_tgt=act_tgt):
                    sy.wait_ge(SS["ve"], act_tgt)
                    sy.dma_start(f2dbg_d[:, 0:2 * IFF], actm[:]
                                 ).then_inc(SS["d_dbg"], 16)
                    sy.dma_start(f2dbg_d[:, 2 * IFF:3 * IFF], gsil[:]
                                 ).then_inc(SS["d_dbg"], 16)
                sch.on("sync", sy_dbg1)
                sch.inc("d_dbg", 32)
            es_gu.close()

            es_dn = ExitStack()
            ptx = es_dn.enter_context(
                nc.psum_tensor(f"ptx{e}", [128, 4096], BF16))
            pdn = [es_dn.enter_context(
                nc.psum_tensor(f"pdn{e}_{p}", [128, D], F32)) for p in range(2)]
            vbase = sch.get("ve")
            vtx_tgt = {}
            for ii in range(16):
                for pt_i in range(2):
                    iin = ii * 2 + pt_i

                    def t_tx(t, pt_i=pt_i, ii=ii, iin=iin, act_tgt=act_tgt,
                             ptx=ptx, vv=max(vbase, sch.get("ve") - 3)):
                        t.wait_ge(SS["ve"], act_tgt)
                        t.wait_ge(SS["ve"], vv)
                        _mm = t.transpose(ptx[:, (iin % 4) * 1024:
                                              (iin % 4) * 1024 + 128],
                                          actm[:, pt_i * IFF + ii * 128:
                                               pt_i * IFF + (ii + 1) * 128],
                                          ident[:])
                        _mm.then_inc(SS["pe"], 1)
                    sch.on("tensor", t_tx)
                    tx_tgt = sch.inc("pe", 1)

                    def v_tx(v, pt_i=pt_i, ii=ii, iin=iin, tx_tgt=tx_tgt,
                             ptx=ptx):
                        v.wait_ge(SS["pe"], tx_tgt)
                        v.tensor_copy(actf[:, ii * 256 + pt_i * 128:
                                          ii * 256 + (pt_i + 1) * 128],
                                      ptx[:, (iin % 4) * 1024:
                                          (iin % 4) * 1024 + 128])
                        v.drain()
                        v.nop().then_inc(SS["ve"], 1)
                    sch.on("vector", v_tx)
                    vtx_tgt[ii] = sch.inc("ve", 1)
            actf_ve = sch.get("ve")
            if F2DBG and e == 0:
                def sy_dbg2(sy, actf_ve=actf_ve):
                    sy.wait_ge(SS["ve"], actf_ve)
                    sy.dma_start(f2dbg_d[:, 3 * IFF:5 * IFF], actf[:]
                                 ).then_inc(SS["d_dbg"], 16)
                sch.on("sync", sy_dbg2)
                sch.inc("d_dbg", 16)

            for cin in range(16):
                lslot, ltgt = ldinfo["d"][e * 16 + cin]

                def t_d2(t, cin=cin, lslot=lslot, ltgt=ltgt, pdn=pdn,
                         aw=vtx_tgt[cin], vw=sch.get("ve")):
                    t.wait_ge(SS[f"d_d{lslot}"], ltgt)
                    t.wait_ge(SS["ve"], aw)
                    if cin == 0:
                        t.wait_ge(SS["ve"], vw)
                    for pt_i in range(2):
                        for half in range(2):
                            _mm = t.matmul(
                                pdn[pt_i][:, half * 512:(half + 1) * 512],
                                actf[:, cin * 256 + pt_i * 128:
                                     cin * 256 + (pt_i + 1) * 128],
                                dbuf[lslot][:, half * 512:(half + 1) * 512],
                                start=(cin == 0), stop=(cin == 15))
                    _mm.then_inc(SS["pe"], 1)
                sch.on("tensor", t_d2)
                consume("d", sch.inc("pe", 1))
            dn_done = sch.get("pe")

            for pt_i in range(2):
                def v_dn(v, pt_i=pt_i, dn_done=dn_done, pdn=pdn,
                         sw=sch.get("d_dst")):
                    v.wait_ge(SS["pe"], dn_done)
                    v.wait_ge(SS["d_dst"], sw)
                    v.tensor_copy(dst[:], pdn[pt_i][:])
                    v.drain()
                    v.nop().then_inc(SS["ve"], 1)
                sch.on("vector", v_dn)
                dn_ve = sch.inc("ve", 1)

                def sy_dn(sy, e=e, pt_i=pt_i, dn_ve=dn_ve):
                    sy.wait_ge(SS["ve"], dn_ve)
                    sy.dma_start(dexp_d[e * CAP + pt_i * 128:
                                        e * CAP + (pt_i + 1) * 128, :], dst[:]
                                 ).then_inc(SS["d_dst"], 16)
                sch.on("sync", sy_dn)
                sch.inc("d_dst", 16)
            es_dn.close()

    def emit_f14(es_ph, ln2fm):
        ln2fm = [sbt(es_ph, f"f1ln2fm{c}", [128, N], BF16,
                     is_transpose_dest=True) for c in range(8)]

        def s_trl2(s):
            for c in range(8):
                s.dma_start(ln2fm[c][:], ln2_d[:, c * 128:(c + 1) * 128],
                            transpose=True).then_inc(SS["d_trl"], 16)
        sch.on("scalar", s_trl2)
        trl2_tgt = sch.inc("d_trl", 16 * 8)
        shg = [sbt(es_ph, f"shg{c}", [128, IFF], BF16) for c in range(8)]
        shu = [sbt(es_ph, f"shu{c}", [128, IFF], BF16) for c in range(8)]
        shd = [sbt(es_ph, f"shd{c}", [128, D], BF16) for c in range(16)]
        gsil = sbt(es_ph, "f1gsil", [128, IFF], BF16)
        actm = sbt(es_ph, "f1actm", [128, IFF], BF16)
        actf = sbt(es_ph, "f1actf", [128, 16 * 128], BF16)
        rtt = [sbt(es_ph, f"rtt{i}", [128, D], BF16) for i in range(2)]
        xmt = [sbt(es_ph, f"xmt{i}", [128, D], BF16) for i in range(2)]
        fo = [sbt(es_ph, f"fo{i}", [128, D], F32) for i in range(2)]

        def g_ldw(g):
            for c in range(8):
                g.dma_start(shg[c][:], sh_g[c * 128:(c + 1) * 128, :]
                            ).then_inc(SS["d_shw"], 16)
                g.dma_start(shu[c][:], sh_u[c * 128:(c + 1) * 128, :]
                            ).then_inc(SS["d_shw"], 16)
            for c in range(16):
                g.dma_start(shd[c][:], sh_d[c * 128:(c + 1) * 128, :]
                        ).then_inc(SS["d_shw"], 16)
        sch.on("gpsimd", g_ldw)
        w_tgt = sch.inc("d_shw", 16 * 32)

        pg = es_ph.enter_context(nc.psum_tensor("f1pg", [128, 1024], F32))
        pu_ = es_ph.enter_context(nc.psum_tensor("f1pu", [128, 1024], F32))
        ptx = es_ph.enter_context(nc.psum_tensor("f1ptx", [128, 2048], BF16))
        pdn = es_ph.enter_context(nc.psum_tensor("f1pdn", [128, 1024], F32))

        sil_t = {}
        act_t = {}
        gu_g_t = {}
        gu_u_t = {}
        txp_t = {}
        vtx_t = {}
        dnc_t = {}
        fin_t = {}
        out_t = {}

        for tt in range(NTT):
            for h in range(2):
                k = tt * 2 + h
                kp = ((k - 1) // 2, (k - 1) % 2)

                def t_g(t, tt=tt, h=h, w_tgt=w_tgt, lt=trl2_tgt,
                        sw=sil_t.get(((tt * 2 + h - 1) // 2,
                                      (tt * 2 + h - 1) % 2), 0)):
                    if tt == 0 and h == 0:
                        t.wait_ge(SS["d_shw"], w_tgt)
                        t.wait_ge(SS["d_trl"], lt)
                    t.wait_ge(SS["se"], sw)
                    for cin in range(8):
                        for nb in range(2):
                            _mm = t.matmul(
                                pg[:, nb * 512:(nb + 1) * 512],
                                ln2fm[cin][:, tt * 128:(tt + 1) * 128],
                                shg[cin][:, h * 1024 + nb * 512:
                                          h * 1024 + (nb + 1) * 512],
                                start=(cin == 0), stop=(cin == 7))
                    _mm.then_inc(SS["pe"], 1)
                sch.on("tensor", t_g)
                gu_g_t[(tt, h)] = sch.inc("pe", 1)

                def t_u(t, tt=tt, h=h, aw=act_t.get(kp, 0)):
                    t.wait_ge(SS["ve"], aw)
                    for cin in range(8):
                        for nb in range(2):
                            _mm = t.matmul(
                                pu_[:, nb * 512:(nb + 1) * 512],
                                ln2fm[cin][:, tt * 128:(tt + 1) * 128],
                                shu[cin][:, h * 1024 + nb * 512:
                                          h * 1024 + (nb + 1) * 512],
                                start=(cin == 0), stop=(cin == 7))
                    _mm.then_inc(SS["pe"], 1)
                sch.on("tensor", t_u)
                gu_u_t[(tt, h)] = sch.inc("pe", 1)

                def s_sil(s, tt=tt, h=h, gw=gu_g_t[(tt, h)],
                          aw=act_t.get((tt - 1, h), 0)):
                    s.wait_ge(SS["pe"], gw)
                    s.wait_ge(SS["ve"], aw)
                    s.activation(gsil[:, h * 1024:(h + 1) * 1024], pg[:],
                                 AF.Silu)
                    s.drain()
                    s.nop().then_inc(SS["se"], 1)
                sch.on("scalar", s_sil)
                sil_t[(tt, h)] = sch.inc("se", 1)

                def v_act(v, tt=tt, h=h, st=sil_t[(tt, h)],
                          uw=gu_u_t[(tt, h)], tw=txp_t.get((tt - 1, h), 0)):
                    v.wait_ge(SS["se"], st)
                    v.wait_ge(SS["pe"], uw)
                    v.wait_ge(SS["pe"], tw)
                    v.tensor_tensor(actm[:, h * 1024:(h + 1) * 1024],
                                    gsil[:, h * 1024:(h + 1) * 1024],
                                    pu_[:], op=ALU.mult)
                    v.drain()
                    v.nop().then_inc(SS["ve"], 1)
                sch.on("vector", v_act)
                act_t[(tt, h)] = sch.inc("ve", 1)

            for h in range(2):
                for k in range(8):
                    ii = h * 8 + k
                    q = (tt * 2 + h) * 4 + k // 2

                    def t_tx(t, tt=tt, h=h, k=k, ii=ii,
                             aw=act_t[(tt, h)],
                             vv=vtx_t.get(q - 1, 0)):
                        t.wait_ge(SS["ve"], aw)
                        t.wait_ge(SS["ve"], vv)
                        _mm = t.transpose(ptx[:, (k % 2) * 1024:
                                              (k % 2) * 1024 + 128],
                                          actm[:, ii * 128:(ii + 1) * 128],
                                          ident[:])
                        _mm.then_inc(SS["pe"], 1)
                    sch.on("tensor", t_tx)
                    tx_tgt = sch.inc("pe", 1)
                    if k % 2 == 1:
                        def v_tx(v, tt=tt, h=h, k=k, ii=ii, tx_tgt=tx_tgt,
                                 dw=dnc_t.get((tt - 1, ii), 0)):
                            v.wait_ge(SS["pe"], tx_tgt)
                            v.wait_ge(SS["pe"], dw)
                            v.tensor_copy(
                                actf[:, (ii - 1) * 128:(ii + 1) * 128]
                                .rearrange("p (a c) -> p a c", a=2),
                                ptx.rearrange("p (a c) -> p a c", a=2)
                                [:, :, 0:128])
                            v.drain()
                            v.nop().then_inc(SS["ve"], 1)
                        sch.on("vector", v_tx)
                        vtx_t[q] = sch.inc("ve", 1)
                txp_t[(tt, h)] = sch.get("pe")

            def g_ga(g, tt=tt, fw=fin_t.get(tt - 2, 0),
                     dw=sch.get("d_dst")):
                g.wait_ge(SS["ve"], fw)
                if tt == 0:
                    g.wait_ge(SS["d_dst"], dw)
                g.indirect_dma_start(
                    out=rtt[tt % 2][:], out_offset=None,
                    in_=dexp_d,
                    in_offset=IndirectOffsetOnAxis(ap=p_all[:, tt:tt + 1],
                                                   axis=0)
                    ).then_inc(SS[f"d_ga{tt % 2}"], 16)
            sch.on("gpsimd", g_ga)
            ga_tgt = sch.inc(f"d_ga{tt % 2}", 16)

            def sy_xm(sy, tt=tt, fw=fin_t.get(tt - 2, 0)):
                sy.wait_ge(SS["ve"], fw)
                sy.dma_start(xmt[tt % 2][:], xmid_d[tt * 128:(tt + 1) * 128, :]
                            ).then_inc(SS[f"d_xt{tt % 2}"], 16)
            sch.on("sync", sy_xm)
            xm_tgt = sch.inc(f"d_xt{tt % 2}", 16)

            for cin in range(16):
                def t_dn(t, tt=tt, cin=cin,
                         vw=vtx_t[(tt * 2 + cin // 8) * 4 + (cin % 8) // 2],
                         fw=fin_t.get(tt - 1, 0)):
                    t.wait_ge(SS["ve"], vw)
                    if cin == 0:
                        t.wait_ge(SS["ve"], fw)
                    _mm = t.matmul(pdn[:, 0:512],
                                   actf[:, cin * 128:(cin + 1) * 128],
                                   shd[cin][:, 0:512],
                                   start=(cin == 0), stop=(cin == 15))
                    _mm = t.matmul(pdn[:, 512:1024],
                                   actf[:, cin * 128:(cin + 1) * 128],
                                   shd[cin][:, 512:1024],
                                   start=(cin == 0), stop=(cin == 15))
                    _mm.then_inc(SS["pe"], 1)
                sch.on("tensor", t_dn)
                dnc_t[(tt, cin)] = sch.inc("pe", 1)
            dn_done = sch.get("pe")

            def v_fin(v, tt=tt, dn_done=dn_done, ga_tgt=ga_tgt,
                      xm_tgt=xm_tgt, ow=out_t.get(tt - 2, 0)):
                v.wait_ge(SS["pe"], dn_done)
                v.wait_ge(SS[f"d_ga{tt % 2}"], ga_tgt)
                v.wait_ge(SS[f"d_xt{tt % 2}"], xm_tgt)
                v.wait_ge(SS[f"d_out{tt % 2}"], ow)
                v.scalar_tensor_tensor(fo[tt % 2][:], pdn[:], 1.0,
                                       rtt[tt % 2][:],
                                       op0=ALU.mult, op1=ALU.add)
                v.drain()
                v.tensor_tensor(fo[tt % 2][:], fo[tt % 2][:], xmt[tt % 2][:],
                                op=ALU.add)
                v.drain()
                v.nop().then_inc(SS["ve"], 1)
            sch.on("vector", v_fin)
            fin_t[tt] = sch.inc("ve", 1)

            def sy_out(sy, tt=tt, fin_tgt=fin_t[tt]):
                sy.wait_ge(SS["ve"], fin_tgt)
                sy.dma_start(out_d[tt * 128:(tt + 1) * 128, :], fo[tt % 2][:]
                            ).then_inc(SS[f"d_out{tt % 2}"], 16)
            sch.on("sync", sy_out)
            out_t[tt] = sch.inc(f"d_out{tt % 2}", 16)

    order = ["init", "p0", "p1", "p2", "f0a", "f0", "f2", "f14"]
    lim = order.index(upto) if upto in order else len(order)

    def want(ph):
        return order.index(ph) <= lim if upto != "all" else True
    with ExitStack() as es_i:
        emit_init(es_i)
        sch.barrier()
    if want("p0"):
        with ExitStack() as es0:
            emit_ln_pass(es0, "p0", x_in, ln1_d, F32)
            sch.barrier()
    if want("p1"):
        with ExitStack() as es_at:
            vtm = sbt(es_at, "vtm", [128, NJT * 1040], BF16)
            with ExitStack() as es1:
                emit_p1(es1, vtm)
                sch.barrier()
            if want("p2"):
                with ExitStack() as es2:
                    emit_p2(es2, vtm)
                    sch.barrier()
    if want("f0a"):
        with ExitStack() as esf:
            with ExitStack() as esl:
                emit_ln_pass(esl, "f0", xmid_d, ln2_d, BF16)
                sch.barrier()
            if want("f0"):
                with ExitStack() as esr:
                    ln2fm = [sbt(esr, f"ln2fm{c}", [128, N], BF16,
                                 is_transpose_dest=True) for c in range(8)]
                    emit_f0(esr, ln2fm)
                    sch.barrier()
            if want("f2"):
                with ExitStack() as ese:
                    emit_f2(ese, None)
                    sch.barrier()
            if want("f14"):
                with ExitStack() as es4:
                    emit_f14(es4, None)

    dma_targets = {nm: sch.get(nm) for nm in dma_sem_names if sch.get(nm)}

    def sy_fin(sy, dma_targets=dma_targets):
        for nm, v in dma_targets.items():
            sy.wait_ge(SS[nm], v)
    sch.on("sync", sy_fin)

    with nc.Block() as blk:
        sch.emit(blk)
    es_top.close()
    return nc


_NC_CACHE = {}


def kernel(**inputs):
    if "main" not in _NC_CACHE:
        _NC_CACHE["main"] = build_nc()
    nc = _NC_CACHE["main"]
    consts = host_consts()
    x = np.ascontiguousarray(inputs["L0_attn_x"])
    in_maps = []
    for core in range(NCORE):
        m = {
            "x": np.ascontiguousarray(x[core]),
            "Wq": inputs["L0_attn_Wq"], "Wk": inputs["L0_attn_Wk"],
            "Wv": inputs["L0_attn_Wv"], "Wproj": inputs["L0_attn_Wproj"],
            "bproj": inputs["L0_attn_bproj"], "Wpre": inputs["L0_attn_Wpre"],
            "bpre": inputs["L0_attn_bpre"], "pi": inputs["L0_attn_pi"],
            "gate_w": inputs["L0_ffn_gate_w"],
            "sh_g": inputs["L0_ffn_shared_gate_w"],
            "sh_u": inputs["L0_ffn_shared_up_w"],
            "sh_d": inputs["L0_ffn_shared_down_w"],
            "r_g": inputs["L0_ffn_routed_gate_w"],
            "r_u": inputs["L0_ffn_routed_up_w"],
            "r_d": inputs["L0_ffn_routed_down_w"],
        }
        for k, v in consts.items():
            m["c_" + k] = v
        in_maps.append(m)
    res = run_bass_kernel_spmd(nc, in_maps, core_ids=list(range(NCORE)))
    out = np.stack([r["out"] for r in res.results], axis=0)
    return out.astype(np.float32)
